# version 14
# baseline (speedup 1.0000x reference)
import numpy as np
import ml_dtypes
from contextlib import ExitStack

import concourse.bass as bass
import concourse.mybir as mybir
from concourse.bass_utils import run_bass_kernel_spmd

F32 = mybir.dt.float32
BF16 = mybir.dt.bfloat16
AF = mybir.ActivationFunctionType
ALU = mybir.AluOpType
AX = mybir.AxisListType.X
bf16 = ml_dtypes.bfloat16

B, L, D = 2, 4096, 2048
INNER = 4096
T = 1024
HALO = 4
TC = T + HALO
NK = D // 128
NCB = INNER // 128
NO = D // 128
FI = 8192
NIB = FI // 128
NCHUNK = 4
IBC = NIB // NCHUNK
RI = 1024
NRIB = RI // 128
NEG = -1.0e30
EPS = 1e-5


class Prog:

    def __init__(self):
        self.ops = []
        self.cnt = {}

    def val(self, ch):
        return self.cnt.get(ch, 0)

    def op(self, eng, fn=None, waits=(), inc=None, k=1):
        w = tuple((c, v) for (c, v) in waits if v and v > 0)
        self.ops.append((eng, fn, w, inc, k))
        if inc is not None:
            self.cnt[inc] = self.cnt.get(inc, 0) + k
            return self.cnt[inc]
        return None

    def dma(self, eng, fn, waits=(), inc=None):
        return self.op(eng, fn, waits, inc, 16)


_ACT_FALLBACK = {"silu_and_others": 18, "natural_log_exp_and_others": 6, "sqrt_and_others": 3}


def _act_set_id(name):
    try:
        from concourse.hw_specs import get_activation_tables
        keys = list(get_activation_tables("TRN2").keys())
        return keys.index(name)
    except Exception:
        pass
    try:
        import json as _json
        import os as _os
        import neuronxcc
        p = _os.path.join(_os.path.dirname(neuronxcc.__file__), "pwp", "pwp_bin_trainium", "act_info.json")
        sets = _json.load(open(p))["act_func_sets"]
        return [s["name"] for s in sets].index(name)
    except Exception:
        return _ACT_FALLBACK[name]


def build_nc(dbg=False):
    nc = bass.Bass("TRN2", target_bir_lowering=False, debug=False)
    ikind = {"kind": "ExternalOutput"} if dbg else {}

    dx = nc.dram_tensor("x", [D, TC], F32, kind="ExternalInput").ap()
    dwin = nc.dram_tensor("winT", [D, INNER], F32, kind="ExternalInput").ap()
    dwg = nc.dram_tensor("wgT", [D, INNER], F32, kind="ExternalInput").ap()
    dwo = nc.dram_tensor("woT", [INNER, D], F32, kind="ExternalInput").ap()
    dwsg = nc.dram_tensor("wsgT", [D, FI], F32, kind="ExternalInput").ap()
    dwsu = nc.dram_tensor("wsuT", [D, FI], F32, kind="ExternalInput").ap()
    dwsd = nc.dram_tensor("wsdT", [FI, D], F32, kind="ExternalInput").ap()
    dwrg = nc.dram_tensor("wrgT", [D, RI], F32, kind="ExternalInput").ap()
    dwru = nc.dram_tensor("wruT", [D, RI], F32, kind="ExternalInput").ap()
    dwrd = nc.dram_tensor("wrdT", [RI, D], F32, kind="ExternalInput").ap()
    dwrt = nc.dram_tensor("wrtT", [D, 8], F32, kind="ExternalInput").ap()
    dsdt = nc.dram_tensor("sdt", [128, NCB * 128], F32, kind="ExternalInput").ap()
    ddiag = nc.dram_tensor("diag", [128, NCB * 4 * 128], F32, kind="ExternalInput").ap()
    dalog = nc.dram_tensor("alog", [128, NCB], F32, kind="ExternalInput").ap()
    dbp = nc.dram_tensor("bp", [128, NCB], F32, kind="ExternalInput").ap()
    dcp = nc.dram_tensor("cp", [128, NCB], F32, kind="ExternalInput").ap()
    ddp = nc.dram_tensor("dp", [128, NCB], F32, kind="ExternalInput").ap()
    dconvb = nc.dram_tensor("convb", [128, NCB], F32, kind="ExternalInput").ap()
    ddtb = nc.dram_tensor("dtbc", [128, NCB], F32, kind="ExternalInput").ap()
    dcmask = nc.dram_tensor("cmask", [128, 8], F32, kind="ExternalInput").ap()
    dident = nc.dram_tensor("ident", [128, 128], F32, kind="ExternalInput").ap()
    dout = nc.dram_tensor("o", [D, T], F32, kind="ExternalOutput").ap()

    dspB = nc.dram_tensor("spillB", [2, NCB, 128, 512], BF16, **ikind)
    dspG = nc.dram_tensor("spillG", [2, NCB, 128, 512], BF16, **ikind)
    dx2 = nc.dram_tensor("x2sp", [D, T], F32, **ikind)
    defin = nc.dram_tensor("efin", [128, 64], F32)
    defout = nc.dram_tensor("efout", [8, 128, 64], F32)
    ddbg = nc.dram_tensor("dbg", [128, 8192], F32, kind="ExternalOutput").ap() if dbg else None
    ddbg2 = nc.dram_tensor("dbg2", [128, 2048], BF16, kind="ExternalOutput").ap() if dbg else None

    vwin = dwin.rearrange("(k p) c -> p k c", p=128)
    vwg = dwg.rearrange("(k p) c -> p k c", p=128)
    vwo = dwo.rearrange("(k p) c -> p k c", p=128)
    vwsg = dwsg.rearrange("(k p) c -> p k c", p=128)
    vwsu = dwsu.rearrange("(k p) c -> p k c", p=128)
    vwsd = dwsd.rearrange("(k p) c -> p k c", p=128)
    vwrg = dwrg.rearrange("(k p) c -> p k c", p=128)
    vwru = dwru.rearrange("(k p) c -> p k c", p=128)
    vwrd = dwrd.rearrange("(k p) c -> p k c", p=128)
    vwrt = dwrt.rearrange("(k p) c -> p k c", p=128)
    vx = dx.rearrange("(k p) t -> p k t", p=128)

    es = ExitStack()
    sb = lambda name, shape, dt: es.enter_context(nc.sbuf_tensor(name, shape, dt))
    psb = lambda name, shape: es.enter_context(nc.psum_tensor(name, shape, F32))

    actb = sb("actb", [128, NK * TC], BF16)
    bigb = sb("bigb", [128, 24640], BF16)
    bigf = sb("bigf", [128, 16384], F32)
    wstage = sb("wstage", [128, 4096], F32)
    wcast = sb("wcast", [128, 8192], BF16)
    strips = sb("strips", [128, 2048], F32)
    smallf = sb("smallf", [128, 1952], F32)
    identf = sb("identf", [128, 128], F32)
    comb8T = sb("comb8T", [8, 128], F32)
    ones1f = sb("ones1f", [1, 128], F32)
    onesbf = sb("onesbf", [128, 1], BF16)
    wrtb = sb("wrtb", [128, 128], BF16)
    rows = sb("rows", [1, 2048], F32)
    basebf = sb("basebf", [128, 1024], BF16)
    gbf = sb("gbf", [128, 1024], BF16)

    psA = psb("psA", [128, 1024])
    psG = psb("psG", [128, 1024])
    psC = psb("psC", [128, 1024])
    psD = psb("psD", [128, 1024])

    ACOL = smallf[:, 0:32]
    BPC = smallf[:, 32:64]
    CPH = smallf[:, 64:96]
    DPH = smallf[:, 96:128]
    CVB = smallf[:, 128:160]
    DTB = smallf[:, 160:192]
    CMASK = smallf[:, 192:200]
    ONESA = smallf[:, 200:232]
    CARRY = smallf[:, 232:264]
    LOGIT = smallf[:, 264:328]
    COMB8 = smallf[:, 328:336]
    EFSB = smallf[:, 336:400]
    GATH = smallf[:, 400:912]
    CBC = smallf[:, 912:1936]
    SCR = smallf[:, 1936:1952]

    ET = bigf[:, 10240:11264]
    EP1 = bigf[:, 11264:12288]
    SLOC = bigf[:, 12288:13312]
    PP = bigf[:, 13312:14336]
    DRV = bigf[:, 14336:15360]
    Y1 = bigf[:, 0:16384]
    ACC = bigf[:, 0:16384]

    DIAGB = bigb[:, 0:16384]
    SDTB = bigb[:, 16384:20480]
    PROJB = bigb[:, 20480:21508]
    XB = bigb[:, 21536:23584]
    COMBINED = bigb[:, 0:16384]
    BLB8 = bigb[:, 16384:20480]
    BLG8 = bigb[:, 20480:24576]
    STATB = bigb[:, 21536:22560]
    HH = bigb[:, 0:16384]
    HH2 = bigb[:, 16384:24576]

    ZEROS = strips[:, 0:1024]

    SET_SILU = _act_set_id("silu_and_others")
    SET_NLEXP = _act_set_id("natural_log_exp_and_others")
    SET_SQRT = _act_set_id("sqrt_and_others")

    def actload(set_id):
        def fn(s):
            return s.add_instruction(mybir.InstLoadActFuncSet(
                name=nc.get_next_instruction_name(), act_func_set_id=set_id,
                ins=[], outs=[]))
        return fn

    p = Prog()
    W, G, PE, SC, VE, CC, OUT = "wld", "gld", "pe", "sc", "ve", "cc", "out"

    p.op("gpsimd", lambda e: e.collective_compute(
        "AllGather", ALU.bypass, replica_groups=[list(range(8))],
        ins=[defin.ap()], outs=[defout.ap()]), inc=CC)
    for ap_dst, ap_src in ((smallf[:, 0:32], dalog), (BPC, dbp), (smallf[:, 64:96], dcp),
                           (smallf[:, 96:128], ddp), (CVB, dconvb), (DTB, ddtb),
                           (CMASK, dcmask), (identf[:], dident)):
        p.dma("sync", (lambda d, s: lambda e: e.dma_start(out=d, in_=s))(ap_dst, ap_src), inc=W)
    v_small = p.val(W)
    p.dma("sync", lambda e: e.dma_start(out=wstage[:, 0:128], in_=vwrt), inc=W)
    v_wrt = p.val(W)
    p.op("scalar", lambda e: e.copy(out=wrtb[:], in_=wstage[:, 0:128]), waits=[(W, v_wrt)], inc=SC)
    p.op("vector", lambda e: e.memset(ONESA, 1.0), inc=VE)
    p.op("vector", lambda e: e.memset(onesbf[:], 1.0), inc=VE)
    p.op("vector", lambda e: e.memset(ones1f[:], 1.0), inc=VE)
    p.op("vector", lambda e: e.memset(ZEROS, 0.0), inc=VE)
    p.op("scalar", lambda e: e.activation(out=LOGIT[:, 0:32], in_=smallf[:, 0:32], func=AF.Exp),
         waits=[(W, v_small)], inc=SC)
    v_sc0 = p.val(SC)
    p.op("vector", lambda e: e.tensor_scalar_mul(out=ACOL, in0=LOGIT[:, 0:32], scalar1=-1.0),
         waits=[(SC, v_sc0)], inc=VE)
    p.op("vector", lambda e: e.tensor_scalar_mul(out=CPH, in0=smallf[:, 64:96], scalar1=0.5),
         waits=[(W, v_small)], inc=VE)
    p.op("vector", lambda e: e.tensor_scalar_mul(out=DPH, in0=smallf[:, 96:128], scalar1=0.5),
         waits=[(W, v_small)], inc=VE)
    v_const = p.val(VE)

    stage_jobs = []
    for k in range(NK):
        stage_jobs.append((vx[:, k, :], TC, actb[:, k * TC:(k + 1) * TC]))
    for j in range(8):
        stage_jobs.append((ddiag[:, j * 2048:(j + 1) * 2048], 2048, DIAGB[:, j * 2048:(j + 1) * 2048]))
    for j in range(2):
        stage_jobs.append((dsdt[:, j * 2048:(j + 1) * 2048], 2048, SDTB[:, j * 2048:(j + 1) * 2048]))
    wrt_cast = p.val(SC)
    qrr = ["sync", "gpsimd", "scalar"]
    cast_done = {}
    for ci, (src, w_, dst) in enumerate(stage_jobs):
        slot = ci % 2
        wait_cast = cast_done[ci - 2] if ci >= 2 else (SC, wrt_cast)
        v = p.dma(qrr[ci % 3], (lambda s_, w2, sl: lambda e: e.dma_start(out=wstage[:, sl * 2048: sl * 2048 + w2], in_=s_))(src, w_, slot),
                  waits=[wait_cast], inc=W)
        if ci % 2 == 0:
            cast_done[ci] = (SC, p.op("scalar", (lambda d_, w2, sl: lambda e: e.copy(out=d_, in_=wstage[:, sl * 2048: sl * 2048 + w2]))(dst, w_, slot),
                                      waits=[(W, v)], inc=SC))
        else:
            cast_done[ci] = (VE, p.op("vector", (lambda d_, w2, sl: lambda e: e.tensor_copy(out=d_, in_=wstage[:, sl * 2048: sl * 2048 + w2]))(dst, w_, slot),
                                      waits=[(W, v)], inc=VE))

    def w_slice(view, cb_, width=128):
        return view[:, :, cb_ * width:(cb_ + 1) * width]

    nsj = len(stage_jobs)
    v_setup_ve = p.val(VE)
    vv = p.dma("sync", (lambda: lambda e: e.dma_start(out=wstage[:, 0:2048], in_=w_slice(vwin, 0)))(),
               waits=[cast_done[nsj - 2]], inc=W)
    vv2 = p.dma("sync", (lambda: lambda e: e.dma_start(out=wstage[:, 2048:4096], in_=w_slice(vwg, 0)))(),
                waits=[cast_done[nsj - 1]], inc=W)
    c_win = p.op("scalar", lambda e: e.copy(out=wcast[:, 0:2048], in_=wstage[:, 0:2048]), waits=[(W, vv)], inc=SC)
    c_wg = p.op("scalar", lambda e: e.copy(out=wcast[:, 2048:4096], in_=wstage[:, 2048:4096]), waits=[(W, vv2)], inc=SC)

    A = {}

    def rec(cb_, key, v):
        A[(cb_, key)] = v

    def get(cb_, key, default=0):
        return A.get((cb_, key), default)

    def w_slice2(view, cb_):
        return view[:, :, cb_ * 128:(cb_ + 1) * 128]

    rec(0, "cast_w", (SC, c_win))
    rec(0, "cast_g", (SC, c_wg))
    if NCB > 1:
        vs1 = p.dma("sync", lambda e: e.dma_start(out=wstage[:, 0:2048], in_=w_slice2(vwin, 1)),
                    waits=[(SC, c_wg)], inc=W)
        vs2 = p.dma("sync", lambda e: e.dma_start(out=wstage[:, 2048:4096], in_=w_slice2(vwg, 1)), inc=W)
        rec(1, "staged", (vs1, vs2))

    for i in range(NCB + 2):
        j1 = i - 1
        j2 = i - 2
        par_i = i % 2
        if i + 1 <= NCB - 1 and (i + 1, "staged") in A and (i + 1, "cast_w") not in A:
            s1, s2 = A[(i + 1, "staged")]
            npar = (i + 1) % 2
            cwv = p.op("scalar", (lambda sl: lambda e: e.copy(
                out=wcast[:, (2 * sl) * 2048:(2 * sl) * 2048 + 2048], in_=wstage[:, 0:2048]))(npar),
                waits=[(W, s1), (PE, get(i - 1, "pe_gate"))], inc=SC)
            cgv = p.op("scalar", (lambda sl: lambda e: e.copy(
                out=wcast[:, (2 * sl + 1) * 2048:(2 * sl + 1) * 2048 + 2048], in_=wstage[:, 2048:4096]))(npar),
                waits=[(W, s2)], inc=SC)
            rec(i + 1, "cast_w", (SC, cwv))
            rec(i + 1, "cast_g", (SC, cgv))
        if i < NCB:
            winb = wcast[:, (2 * par_i) * 2048:(2 * par_i) * 2048 + 2048]
            wgb = wcast[:, (2 * par_i + 1) * 2048:(2 * par_i + 1) * 2048 + 2048]
            hw = [A[(i, "cast_w")], (SC, get(i - 3, "e")), (SC, get(i - 1, "evacp"))]
            if i == 0:
                hw.append((VE, v_setup_ve))
            for k in range(NK):
                p.op("tensor", (lambda k_, wb_: lambda e: e.matmul(
                    psD[:, 0:4], wb_[:, k_ * 128:(k_ + 1) * 128],
                    actb[:, k_ * TC: k_ * TC + HALO],
                    start=(k_ == 0), stop=(k_ == NK - 1)))(k, winb),
                    waits=hw if k == 0 else (), inc=PE if k == NK - 1 else None)
            for k in range(NK):
                p.op("tensor", (lambda k_, wb_: lambda e: e.matmul(
                    psA[:, 0:512], wb_[:, k_ * 128:(k_ + 1) * 128],
                    actb[:, k_ * TC + HALO: k_ * TC + HALO + 512],
                    start=(k_ == 0), stop=(k_ == NK - 1)))(k, winb), inc=PE if k == NK - 1 else None)
                p.op("tensor", (lambda k_, wb_: lambda e: e.matmul(
                    psA[:, 512:1024], wb_[:, k_ * 128:(k_ + 1) * 128],
                    actb[:, k_ * TC + HALO + 512: k_ * TC + TC],
                    start=(k_ == 0), stop=(k_ == NK - 1)))(k, winb), inc=PE if k == NK - 1 else None)
            rec(i, "pe_proj", p.val(PE))
            gw = [A[(i, "cast_g")], (SC, get(i - 1, "thg"))]
            for k in range(NK):
                p.op("tensor", (lambda k_, wb_: lambda e: e.matmul(
                    psG[:, 0:512], wb_[:, k_ * 128:(k_ + 1) * 128],
                    actb[:, k_ * TC + HALO: k_ * TC + HALO + 512],
                    start=(k_ == 0), stop=(k_ == NK - 1)))(k, wgb),
                    waits=gw if k == 0 else (), inc=PE if k == NK - 1 else None)
                p.op("tensor", (lambda k_, wb_: lambda e: e.matmul(
                    psG[:, 512:1024], wb_[:, k_ * 128:(k_ + 1) * 128],
                    actb[:, k_ * TC + HALO + 512: k_ * TC + TC],
                    start=(k_ == 0), stop=(k_ == NK - 1)))(k, wgb), inc=PE if k == NK - 1 else None)
            rec(i, "pe_gate", p.val(PE))
        if 0 <= j1 < NCB:
            cw = [(SC, get(j1, "evacp")), (SC, get(j1 - 1, "X"))]
            for tt in range(2):
                for k in range(4):
                    p.op("tensor", (lambda tt_, k_, cb_: lambda e: e.matmul(
                        psC[:, tt_ * 512:(tt_ + 1) * 512],
                        DIAGB[:, (cb_ * 4 + k_) * 128:(cb_ * 4 + k_ + 1) * 128],
                        PROJB[:, 1 + k_ + tt_ * 512: 513 + k_ + tt_ * 512],
                        start=(k_ == 0), stop=(k_ == 3)))(tt, k, j1),
                        waits=cw if (tt == 0 and k == 0) else (), inc=PE if k == 3 else None)
            rec(j1, "pe_conv", p.val(PE))
        if i < NCB:
            ew = [(PE, get(i, "pe_proj")), (PE, get(i - 1, "pe_conv"))]
            p.op("scalar", lambda e: e.copy(out=PROJB[:, 0:HALO], in_=psD[:, 0:4]), waits=ew, inc=SC)
            rec(i, "evacp", p.op("scalar", lambda e: e.copy(out=PROJB[:, HALO:TC], in_=psA[:, 0:1024]), inc=SC))
        if 0 <= j2 < NCB:
            dtw = [(SC, get(j2, "Xb")), (SC, get(j2 - 1, "e")), (SC, get(min(i, NCB - 1), "evacp"))]
            for tt in range(2):
                p.op("tensor", (lambda tt_, cb_: lambda e: e.matmul(
                    psD[:, tt_ * 512:(tt_ + 1) * 512],
                    SDTB[:, cb_ * 128:(cb_ + 1) * 128],
                    XB[:, (cb_ % 2) * 1024 + tt_ * 512: (cb_ % 2) * 1024 + (tt_ + 1) * 512],
                    start=True, stop=True))(tt, j2),
                    waits=dtw if tt == 0 else (), inc=PE if tt == 1 else None)
            rec(j2, "pe_dt", p.val(PE))
        if 0 <= j2 < NCB:
            rec(j2, "e", p.op("scalar", (lambda cb_: lambda e: e.activation(
                out=ET, in_=psD[:, 0:1024], func=AF.Exp, bias=DTB[:, cb_:cb_ + 1], scale=1.0))(j2),
                waits=[(PE, get(j2, "pe_dt")), (VE, get(j2 - 1, "ve_basev"))], inc=SC))
            rec(j2, "ep1", p.op("vector", lambda e: e.tensor_scalar(
                out=EP1, in0=ET, scalar1=1.0, scalar2=None, op0=ALU.add),
                waits=[(SC, get(j2, "e"))], inc=VE))
            dtb_off = 6144 + (j2 % 2) * 1024
            dec_off = 8192 + (j2 % 2) * 1024
            p.op("scalar", (lambda off: lambda e: e.activation(
                out=bigf[:, off: off + 1024], in_=EP1, func=AF.Ln))(dtb_off),
                waits=[(VE, get(j2, "ep1")), (VE, get(j2 - 2, "ve_drive"))], inc=SC)
            rec(j2, "decay", p.op("scalar", (lambda doff, soff, cb_: lambda e: e.activation(
                out=bigf[:, doff: doff + 1024], in_=bigf[:, soff: soff + 1024],
                func=AF.Exp, scale=ACOL[:, cb_:cb_ + 1]))(dec_off, dtb_off, j2),
                waits=[(VE, get(j2 - 2, "ve_pp"))], inc=SC))
        if i < NCB:
            th_sl = i % 4
            thw = [(PE, get(i, "pe_gate")), (VE, get(i - 4, "ve_g"))]
            rec(i, "thg", p.op("scalar", (lambda sl: lambda e: e.activation(
                out=bigf[:, sl * 1024: sl * 1024 + 1024], in_=psG[:, 0:1024], func=AF.Tanh, scale=0.5))(th_sl),
                waits=thw, inc=SC))
        if 0 <= j1 < NCB:
            x_sl = 4096 + (j1 % 2) * 1024
            xw = [(PE, get(j1, "pe_conv")), (VE, get(j1 - 2, "ve_basev"))]
            rec(j1, "X", p.op("scalar", (lambda off, cb_: lambda e: e.activation(
                out=bigf[:, off: off + 1024], in_=psC[:, 0:1024],
                func=AF.Silu, bias=CVB[:, cb_:cb_ + 1], scale=1.0))(x_sl, j1), waits=xw, inc=SC))
            rec(j1, "Xb", p.op("scalar", (lambda off, cb_: lambda e: e.copy(
                out=XB[:, (cb_ % 2) * 1024:(cb_ % 2) * 1024 + 1024], in_=bigf[:, off: off + 1024]))(x_sl, j1),
                waits=[(PE, get(j1 - 2, "pe_dt"))], inc=SC))
        if i + 2 <= NCB - 1:
            ws = [A[(i + 1, "cast_g")] if (i + 1, "cast_g") in A else (SC, 0)]
            s1 = p.dma("sync", (lambda cb_: lambda e: e.dma_start(
                out=wstage[:, 0:2048], in_=w_slice2(vwin, cb_)))(i + 2), waits=ws, inc=W)
            s2 = p.dma("sync", (lambda cb_: lambda e: e.dma_start(
                out=wstage[:, 2048:4096], in_=w_slice2(vwg, cb_)))(i + 2), inc=W)
            rec(i + 2, "staged", (s1, s2))
        if 0 <= j2 < NCB:
            dtb_t = bigf[:, dtb_off: dtb_off + 1024]
            dec_t = bigf[:, dec_off: dec_off + 1024]
            x_t = bigf[:, 4096 + (j2 % 2) * 1024: 4096 + (j2 % 2) * 1024 + 1024]
            th_t = bigf[:, (j2 % 4) * 1024: (j2 % 4) * 1024 + 1024]
            rec(j2, "ve_drive", p.op("vector", (lambda dtb_, x_, cb_: lambda e: e.scalar_tensor_tensor(
                out=DRV, in0=dtb_, scalar=BPC[:, cb_:cb_ + 1], in1=x_, op0=ALU.mult, op1=ALU.mult))(dtb_t, x_t, j2),
                waits=[(SC, get(j2, "decay"))], inc=VE))
            p.op("vector", (lambda dec_: lambda e: e.tensor_tensor_scan(
                out=SLOC, data0=dec_, data1=DRV, initial=0.0, op0=ALU.mult, op1=ALU.add))(dec_t), inc=VE)
            rec(j2, "ve_pp", p.op("vector", (lambda dec_: lambda e: e.tensor_tensor_scan(
                out=PP, data0=dec_, data1=ZEROS, initial=1.0, op0=ALU.mult, op1=ALU.add))(dec_t), inc=VE))
            p.op("vector", (lambda cb_: lambda e: e.tensor_copy(out=EFSB[:, cb_:cb_ + 1], in_=PP[:, 1023:1024]))(j2), inc=VE)
            p.op("vector", (lambda cb_: lambda e: e.tensor_copy(out=EFSB[:, 32 + cb_:32 + cb_ + 1], in_=SLOC[:, 1023:1024]))(j2), inc=VE)
            p.op("vector", (lambda cb_: lambda e: e.tensor_scalar_mul(out=EP1, in0=SLOC, scalar1=CPH[:, cb_:cb_ + 1]))(j2), inc=VE)
            rec(j2, "ve_basev", p.op("vector", (lambda x_, cb_: lambda e: e.scalar_tensor_tensor(
                out=ET, in0=x_, scalar=DPH[:, cb_:cb_ + 1], in1=EP1, op0=ALU.mult, op1=ALU.add))(x_t, j2), inc=VE))
            rec(j2, "ve_base", p.op("vector", (lambda th_: lambda e: e.scalar_tensor_tensor(
                out=basebf[:], in0=th_, scalar=1.0, in1=ET, op0=ALU.add, op1=ALU.mult))(th_t),
                waits=[(G, get(j2 - 1, "spillB"))], inc=VE))
            p.op("vector", (lambda cb_: lambda e: e.tensor_scalar_mul(out=EP1, in0=PP, scalar1=CPH[:, cb_:cb_ + 1]))(j2), inc=VE)
            rec(j2, "ve_g", p.op("vector", (lambda th_: lambda e: e.scalar_tensor_tensor(
                out=gbf[:], in0=th_, scalar=1.0, in1=EP1, op0=ALU.add, op1=ALU.mult))(th_t),
                waits=[(G, get(j2 - 1, "spillG"))], inc=VE))
            for tt in range(2):
                p.dma("gpsimd", (lambda tt_, cb_: lambda e: e.dma_start(
                    out=dspB.ap()[tt_, cb_], in_=basebf[:, tt_ * 512:(tt_ + 1) * 512]))(tt, j2),
                    waits=[(VE, get(j2, "ve_base"))] if tt == 0 else (), inc=G)
            rec(j2, "spillB", p.val(G))
            for tt in range(2):
                p.dma("gpsimd", (lambda tt_, cb_: lambda e: e.dma_start(
                    out=dspG.ap()[tt_, cb_], in_=gbf[:, tt_ * 512:(tt_ + 1) * 512]))(tt, j2),
                    waits=[(VE, get(j2, "ve_g"))] if tt == 0 else (), inc=G)
            rec(j2, "spillG", p.val(G))

    A_pe = p.val(PE)
    A_sc = p.val(SC)
    A_ve = p.val(VE)
    if dbg:
        for (d0, src_) in ((1184, bigf[:, 9216:10240]), (2208, bigf[:, 7168:8192]),
                           (3232, bigf[:, 5120:6144]), (4256, bigf[:, 3072:4096]),
                           (7328, strips[:, 0:512]), (7840, smallf[:, 0:232])):
            w_ = src_.shape[1]
            p.dma("gpsimd", (lambda a, s_, w2: lambda e: e.dma_start(
                out=ddbg[:, a:a + w2], in_=s_))(d0, src_, w_), waits=[(VE, A_ve)], inc=G)
        p.dma("gpsimd", lambda e: e.dma_start(out=ddbg2[:, 0:1028], in_=PROJB), waits=[(VE, A_ve), (SC, A_sc)], inc=G)
    A_g = p.val(G)

    v = p.dma("gpsimd", lambda e: e.dma_start(out=defin.ap(), in_=EFSB), waits=[(VE, A_ve)], inc=G)
    cc1 = p.op("gpsimd", lambda e: e.collective_compute(
        "AllGather", ALU.bypass, replica_groups=[list(range(8))],
        ins=[defin.ap()], outs=[defout.ap()]), waits=[(G, v)], inc=CC)
    vg2 = p.dma("gpsimd", lambda e: e.dma_start(
        out=GATH, in_=defout.ap().rearrange("j p c -> p j c")), waits=[(CC, cc1)], inc=G)

    p.op("vector", lambda e: e.memset(CARRY, 0.0), waits=[(G, vg2)], inc=VE)
    for j in range(8):
        ej = GATH[:, j * 64: j * 64 + 32]
        fj = GATH[:, j * 64 + 32: j * 64 + 64]
        mj = CMASK[:, j:j + 1]
        p.op("vector", (lambda ej_: lambda e: e.tensor_scalar(
            out=ET[:, 0:32], in0=ej_, scalar1=-1.0, scalar2=None, op0=ALU.add))(ej), inc=VE)
        p.op("vector", lambda e: e.drain(), inc=VE)
        p.op("vector", (lambda mj_: lambda e: e.scalar_tensor_tensor(
            out=ET[:, 64:96], in0=ET[:, 0:32], scalar=mj_, in1=ONESA, op0=ALU.mult, op1=ALU.add))(mj), inc=VE)
        p.op("vector", (lambda fj_, mj_: lambda e: e.tensor_scalar_mul(
            out=ET[:, 32:64], in0=fj_, scalar1=mj_))(fj, mj), inc=VE)
        p.op("vector", lambda e: e.drain(), inc=VE)
        p.op("vector", lambda e: e.tensor_tensor(out=CARRY, in0=CARRY, in1=ET[:, 64:96], op=ALU.mult), inc=VE)
        p.op("vector", lambda e: e.drain(), inc=VE)
        p.op("vector", lambda e: e.tensor_tensor(out=CARRY, in0=CARRY, in1=ET[:, 32:64], op=ALU.add), inc=VE)
        p.op("vector", lambda e: e.drain(), inc=VE)
    p.op("vector", lambda e: e.tensor_copy(out=ET[:, 64:96], in_=CARRY), inc=VE)
    p.op("vector", lambda e: e.tensor_copy(out=ET[:, 64:96], in_=CARRY), inc=VE)
    v_carry = p.val(VE)

    mm_slot_done = {}
    evac_done = {}
    cast_done_c = {}
    stats_pe = 0
    wov = dwo.rearrange("(k p) c -> p k c", p=128)
    fix_done = {}
    wo_cast_prev = A[(NCB - 1, "cast_g")]
    for tt in range(2):
        for r4 in range(8):
            par4 = r4 % 2
            blb = bigb[:, 16384 + par4 * 2048: 16384 + par4 * 2048 + 2048]
            blg = bigb[:, 20480 + par4 * 2048: 20480 + par4 * 2048 + 2048]
            wb_ = [(PE, A_pe), (G, A_g)]
            if tt == 1:
                wb_.append((PE, stats_pe))
                wb_.append((VE, evac_done.get((0, 15), 0)))
            prevkey = (tt, r4 - 2) if r4 >= 2 else ((tt - 1, 6 + r4) if tt > 0 else None)
            if prevkey is not None:
                wb_.append((VE, fix_done.get(prevkey, 0)))
            vB = p.dma("gpsimd", (lambda tt_, r_, d_: lambda e: e.dma_start(
                out=d_, in_=dspB.ap()[tt_, r_ * 4:(r_ + 1) * 4].rearrange("b p t -> p b t")))(tt, r4, blb),
                waits=wb_, inc=G)
            vG2 = p.dma("gpsimd", (lambda tt_, r_, d_: lambda e: e.dma_start(
                out=d_, in_=dspG.ap()[tt_, r_ * 4:(r_ + 1) * 4].rearrange("b p t -> p b t")))(tt, r4, blg), inc=G)
            for i in range(4):
                cb = r4 * 4 + i
                p.op("vector", (lambda i_, cb_, bb_, gg_: lambda e: e.scalar_tensor_tensor(
                    out=COMBINED[:, cb_ * 512:(cb_ + 1) * 512],
                    in0=gg_[:, i_ * 512:(i_ + 1) * 512], scalar=CARRY[:, cb_:cb_ + 1],
                    in1=bb_[:, i_ * 512:(i_ + 1) * 512], op0=ALU.mult, op1=ALU.add))(i, cb, blb, blg),
                    waits=[(G, vG2), (VE, v_carry)] if i == 0 else (), inc=VE)
            fix_done[(tt, r4)] = p.val(VE)
        v_fix = p.val(VE)

        for o in range(NO):
            sw = [wo_cast_prev]
            vh0 = p.dma("sync", (lambda o_: lambda e: e.dma_start(
                out=wstage[:, 0:2048], in_=wov[:, 0:16, o_ * 128:(o_ + 1) * 128]))(o), waits=sw, inc=W)
            vh1 = p.dma("sync", (lambda o_: lambda e: e.dma_start(
                out=wstage[:, 2048:4096], in_=wov[:, 16:32, o_ * 128:(o_ + 1) * 128]))(o), inc=W)
            csl = (o % 2) * 2
            slot_prev = (tt, o - 2) if o >= 2 else (tt - 1, 14 + o)
            cw0 = p.op("vector", (lambda sl: lambda e: e.tensor_copy(
                out=wcast[:, sl * 2048:(sl + 1) * 2048], in_=wstage[:, 0:2048]))(csl),
                waits=[(W, vh0), (PE, mm_slot_done.get(slot_prev, A_pe))], inc=VE)
            cw1 = p.op("vector", (lambda sl: lambda e: e.tensor_copy(
                out=wcast[:, (sl + 1) * 2048:(sl + 2) * 2048], in_=wstage[:, 2048:4096]))(csl),
                waits=[(W, vh1)], inc=VE)
            wo_cast_prev = (VE, cw1)
            cast_done_c[o] = cw1
            vxs = p.dma("gpsimd", (lambda o_, tt_: lambda e: e.dma_start(
                out=strips[:, (o_ % 2) * 512:(o_ % 2) * 512 + 512],
                in_=dx[o_ * 128:(o_ + 1) * 128, HALO + tt_ * 512: HALO + (tt_ + 1) * 512]))(o, tt),
                waits=[(VE, evac_done.get((tt, o - 2), 0) if (tt, o - 2) in evac_done else (evac_done.get((tt - 1, 14 + o), 0)))], inc=G)
            bank = (o % 2) * 512
            mw = [(VE, cw1), (VE, evac_done.get((tt, o - 2), 0))]
            for kk in range(32):
                sl2 = csl * 2048 + (kk // 16) * 2048 + (kk % 16) * 128
                kw = list(mw) if kk == 0 else []
                if kk % 4 == 0:
                    kw.append((VE, fix_done[(tt, kk // 4)]))
                p.op("tensor", (lambda kk_, sl_, bank_, tt_: lambda e: e.matmul(
                    psA[:, bank_:bank_ + 512], wcast[:, sl_: sl_ + 128],
                    COMBINED[:, kk_ * 512:(kk_ + 1) * 512],
                    start=(kk_ == 0), stop=(kk_ == 31)))(kk, sl2, bank, tt),
                    waits=kw, inc=PE if kk == 31 else None)
            mm_slot_done[(tt, o)] = p.val(PE)
            vev = p.op("vector", (lambda o_, tt_, bank_: lambda e: e.tensor_tensor(
                out=Y1[:, o_ * 1024 + tt_ * 512: o_ * 1024 + tt_ * 512 + 512],
                in0=psA[:, bank_:bank_ + 512], in1=strips[:, (o_ % 2) * 512:(o_ % 2) * 512 + 512],
                op=ALU.add))(o, tt, bank),
                waits=[(PE, mm_slot_done[(tt, o)]), (G, vxs)], inc=VE)
            evac_done[(tt, o)] = vev
            sb1 = p.op("scalar", (lambda o_, tt_: lambda e: e.copy(
                out=STATB[:, 0:512], in_=Y1[:, o_ * 1024 + tt_ * 512: o_ * 1024 + tt_ * 512 + 512]))(o, tt),
                waits=[(VE, vev), (PE, stats_pe)], inc=SC)
            sb2 = p.op("scalar", (lambda o_, tt_: lambda e: e.activation(
                out=STATB[:, 512:1024], in_=Y1[:, o_ * 1024 + tt_ * 512: o_ * 1024 + tt_ * 512 + 512],
                func=AF.Square))(o, tt), inc=SC)
            p.op("tensor", (lambda tt_, o_: lambda e: e.matmul(
                psG[0:1, tt_ * 512:(tt_ + 1) * 512], onesbf[:], STATB[:, 0:512],
                start=(o_ == 0), stop=(o_ == NO - 1)))(tt, o), waits=[(SC, sb2)], inc=PE)
            p.op("tensor", (lambda tt_, o_: lambda e: e.matmul(
                psC[0:1, tt_ * 512:(tt_ + 1) * 512], onesbf[:], STATB[:, 512:1024],
                start=(o_ == 0), stop=(o_ == NO - 1)))(tt, o), inc=PE)
            stats_pe = p.val(PE)

    mrow = rows[0:1, 0:1024]
    rstdrow = rows[0:1, 1024:2048]
    tmprow = CBC[0:1, 0:1024]
    v1 = p.op("vector", lambda e: e.tensor_scalar_mul(out=mrow, in0=psG[0:1, 0:1024], scalar1=1.0 / D),
              waits=[(PE, stats_pe)], inc=VE)
    p.op("vector", lambda e: e.tensor_scalar_mul(out=rstdrow, in0=psC[0:1, 0:1024], scalar1=1.0 / D), inc=VE)
    p.op("vector", lambda e: e.tensor_tensor(out=tmprow, in0=mrow, in1=mrow, op=ALU.mult), inc=VE)
    vvar = p.op("vector", lambda e: e.tensor_tensor(out=rstdrow, in0=rstdrow, in1=tmprow, op=ALU.subtract), inc=VE)
    vve = p.op("vector", lambda e: e.tensor_scalar(out=tmprow, in0=rstdrow, scalar1=EPS, scalar2=None, op0=ALU.add),
               waits=[(VE, vvar)], inc=VE)
    vsq = p.op("scalar", lambda e: e.activation(out=rstdrow, in_=tmprow, func=AF.Sqrt),
               waits=[(VE, vve)], inc=SC)
    vrs = p.op("vector", lambda e: e.reciprocal(out=tmprow, in_=rstdrow), waits=[(SC, vsq)], inc=VE)
    bcw = [(VE, vrs), (PE, stats_pe)]
    for tt in range(2):
        p.op("tensor", (lambda tt_: lambda e: e.matmul(
            psD[:, tt_ * 512:(tt_ + 1) * 512], ones1f[:], mrow[0:1, tt_ * 512:(tt_ + 1) * 512],
            start=True, stop=True))(tt), waits=bcw if tt == 0 else (), inc=PE)
        p.op("tensor", (lambda tt_: lambda e: e.matmul(
            psA[:, tt_ * 512:(tt_ + 1) * 512], ones1f[:], tmprow[0:1, tt_ * 512:(tt_ + 1) * 512],
            start=True, stop=True))(tt), inc=PE)
    v_bc = p.val(PE)

    xsp_done = {}
    for o in range(NO):
        for tt in range(2):
            it = o * 2 + tt
            q = it % 4
            w_ = [(PE, v_bc)]
            if it >= 4:
                prev = xsp_done[it - 4]
                w_ += [(G, prev[0]), (SC, prev[1])]
            vu = p.op("vector", (lambda o_, tt_, q_: lambda e: e.tensor_tensor(
                out=strips[:, q_ * 512: q_ * 512 + 512],
                in0=Y1[:, o_ * 1024 + tt_ * 512: o_ * 1024 + tt_ * 512 + 512],
                in1=psD[:, tt_ * 512:(tt_ + 1) * 512], op=ALU.subtract))(o, tt, q), waits=w_, inc=VE)
            vx2 = p.op("vector", (lambda tt_, q_: lambda e: e.tensor_tensor(
                out=strips[:, q_ * 512: q_ * 512 + 512], in0=strips[:, q_ * 512: q_ * 512 + 512],
                in1=psA[:, tt_ * 512:(tt_ + 1) * 512], op=ALU.mult))(tt, q), inc=VE)
            vd = p.dma("gpsimd", (lambda o_, tt_, q_: lambda e: e.dma_start(
                out=dx2[o_ * 128:(o_ + 1) * 128, tt_ * 512:(tt_ + 1) * 512],
                in_=strips[:, q_ * 512: q_ * 512 + 512]))(o, tt, q), waits=[(VE, vx2)], inc=G)
            vc = p.op("scalar", (lambda o_, tt_, q_: lambda e: e.copy(
                out=actb[:, o_ * 1024 + tt_ * 512: o_ * 1024 + tt_ * 512 + 512],
                in_=strips[:, q_ * 512: q_ * 512 + 512]))(o, tt, q), waits=[(VE, vx2)], inc=SC)
            xsp_done[it] = (vd, vc)
    v_x2b = p.val(SC)
    v_x2g = p.val(G)

    ps_all = (psA, psG, psC, psD)
    for tb in range(8):
        pst = ps_all[tb // 2]
        off = (tb % 2) * 512
        for k in range(NK):
            p.op("tensor", (lambda tb_, k_, pst_, off_: lambda e: e.matmul(
                pst_[:, off_:off_ + 8],
                actb[:, k_ * 1024 + tb_ * 128: k_ * 1024 + (tb_ + 1) * 128],
                wrtb[:, k_ * 8:(k_ + 1) * 8],
                start=(k_ == 0), stop=(k_ == NK - 1)))(tb, k, pst, off),
                waits=[(SC, v_x2b)] if (tb == 0 and k == 0) else (), inc=PE if k == NK - 1 else None)
    v_rt = p.val(PE)
    vlg = 0
    for tb in range(8):
        pst = ps_all[tb // 2]
        off = (tb % 2) * 512
        vlg = p.op("vector", (lambda tb_, pst_, off_: lambda e: e.tensor_copy(
            out=LOGIT[:, tb_ * 8:(tb_ + 1) * 8], in_=pst_[:, off_:off_ + 8]))(tb, pst, off),
            waits=[(PE, v_rt)] if tb == 0 else (), inc=VE)
    MX8 = ET[:, 0:64]
    D8 = SCR[:, 0:8]
    S8 = ET[:, 64:72]
    R8 = ET[:, 72:80]
    for tb in range(8):
        p.op("vector", (lambda tb_: lambda e: e.max(
            out=MX8[:, tb_ * 8:(tb_ + 1) * 8], in_=LOGIT[:, tb_ * 8:(tb_ + 1) * 8]))(tb), inc=VE)
    p.op("vector", lambda e: e.drain(), inc=VE)
    for tb in range(8):
        p.op("vector", (lambda tb_: lambda e: e.tensor_tensor(
            out=D8[:, tb_:tb_ + 1], in0=MX8[:, tb_ * 8 + 1:tb_ * 8 + 2],
            in1=MX8[:, tb_ * 8:tb_ * 8 + 1], op=ALU.subtract))(tb), inc=VE)
    p.op("vector", lambda e: e.drain(), inc=VE)
    v_d8 = p.val(VE)
    ve2 = p.op("scalar", lambda e: e.activation(out=SCR[:, 8:16], in_=D8, func=AF.Exp),
               waits=[(VE, v_d8)], inc=SC)
    p.op("vector", lambda e: e.tensor_scalar(out=S8, in0=SCR[:, 8:16], scalar1=1.0,
                                             scalar2=None, op0=ALU.add),
         waits=[(SC, ve2)], inc=VE)
    p.op("vector", lambda e: e.drain(), inc=VE)
    p.op("vector", lambda e: e.reciprocal(out=R8, in_=S8), inc=VE)
    p.op("vector", lambda e: e.drain(), inc=VE)
    p.op("vector", lambda e: e.tensor_tensor(out=COMB8, in0=S8, in1=R8, op=ALU.mult), inc=VE)
    p.op("vector", lambda e: e.drain(), inc=VE)
    v_c8 = p.val(VE)
    vtp = p.op("tensor", lambda e: e.transpose(psA[0:8, 0:128], COMB8, identf[:]),
               waits=[(VE, v_c8)], inc=PE)
    vev8 = p.op("vector", lambda e: e.tensor_copy(out=comb8T[:], in_=psA[0:8, 0:128]),
                waits=[(PE, vtp)], inc=VE)
    vrow = 0
    for tb in range(8):
        vrow = p.dma("gpsimd", (lambda tb_: lambda e: e.dma_start(
            out=mrow[0:1, tb_ * 128:(tb_ + 1) * 128], in_=comb8T[tb_:tb_ + 1, :]))(tb),
            waits=[(VE, vev8)] if tb == 0 else (), inc=G)
    for tt in range(2):
        p.op("tensor", (lambda tt_: lambda e: e.matmul(
            psC[:, tt_ * 512:(tt_ + 1) * 512], ones1f[:], mrow[0:1, tt_ * 512:(tt_ + 1) * 512],
            start=True, stop=True))(tt), waits=[(G, vrow)] if tt == 0 else (), inc=PE)
    v_cbc_pe = p.val(PE)
    p.op("vector", lambda e: e.tensor_copy(out=CBC[:, 0:512], in_=psC[:, 0:512]),
         waits=[(PE, v_cbc_pe)], inc=VE)
    v_cbc = p.op("vector", lambda e: e.tensor_copy(out=CBC[:, 512:1024], in_=psC[:, 512:1024]), inc=VE)

    silu_done = {}
    h_done = {}
    cast_e = {}
    cast_prev = [wo_cast_prev]
    dn_evac = {}
    dn_pe = {}
    slotfree = {0: mm_slot_done[(1, 14)], 1: mm_slot_done[(1, 15)]}
    first_h_wait = [(PE, v_cbc_pe), (VE, v_cbc)]

    psDfree = [0]
    silu_done = {}
    h_done = {}
    dn_evac = {}
    pe_done = {}
    castv = {}
    gu_prev = {0: None, 1: None}
    slotfree_init = {0: mm_slot_done[(1, 14)], 1: mm_slot_done[(1, 15)]}
    first_h_wait = [(PE, v_cbc_pe), (VE, v_cbc)]

    jobs = []
    for c in range(NCHUNK):
        for il in range(IBC):
            jobs.append(("gu", c * IBC + il, vwsg, vwsu))
        if c == NCHUNK - 1:
            for rib in range(NRIB):
                jobs.append(("gu", 1000 + rib, vwrg, vwru))
        for o in range(NO):
            jobs.append(("dn", c, o))

    def emit_job(k):
        if k >= len(jobs):
            return
        stw = [(VE, castv[k - 1])] if k >= 1 else [wo_cast_prev]
        slw = (PE, pe_done.get(k - 2, slotfree_init[k % 2]))
        csl = (k % 2) * 2
        if jobs[k][0] == "gu":
            _, ib, vg, vu = jobs[k]
            wcol = ib - 1000 if ib >= 1000 else ib
            s1 = p.dma("sync", (lambda v_, w_: lambda e: e.dma_start(
                out=wstage[:, 0:2048], in_=v_[:, :, w_ * 128:(w_ + 1) * 128]))(vg, wcol),
                waits=stw, inc=W)
            s2 = p.dma("sync", (lambda v_, w_: lambda e: e.dma_start(
                out=wstage[:, 2048:4096], in_=v_[:, :, w_ * 128:(w_ + 1) * 128]))(vu, wcol), inc=W)
            p.op("vector", (lambda sl: lambda e: e.tensor_copy(
                out=wcast[:, sl * 2048:(sl + 1) * 2048], in_=wstage[:, 0:2048]))(csl),
                waits=[(W, s1), slw], inc=VE)
            castv[k] = p.op("vector", (lambda sl: lambda e: e.tensor_copy(
                out=wcast[:, (sl + 1) * 2048:(sl + 2) * 2048], in_=wstage[:, 2048:4096]))(csl),
                waits=[(W, s2)], inc=VE)
        else:
            _, c, o = jobs[k]
            s1 = p.dma("sync", (lambda c_, o_: lambda e: e.dma_start(
                out=wstage[:, 0:2048], in_=vwsd[:, c_ * 16:(c_ + 1) * 16, o_ * 128:(o_ + 1) * 128]))(c, o),
                waits=stw, inc=W)
            s2 = None
            if c == NCHUNK - 1:
                s2 = p.dma("sync", (lambda o_: lambda e: e.dma_start(
                    out=wstage[:, 2048:3072], in_=vwrd[:, 0:8, o_ * 128:(o_ + 1) * 128]))(o), inc=W)
            cv = p.op("vector", (lambda sl: lambda e: e.tensor_copy(
                out=wcast[:, sl * 2048:(sl + 1) * 2048], in_=wstage[:, 0:2048]))(csl),
                waits=[(W, s1), slw], inc=VE)
            if s2 is not None:
                cv = p.op("vector", (lambda sl: lambda e: e.tensor_copy(
                    out=wcast[:, (sl + 1) * 2048:(sl + 1) * 2048 + 1024], in_=wstage[:, 2048:3072]))(csl),
                    waits=[(W, s2)], inc=VE)
            castv[k] = cv

    stat_pe2 = {}

    def emit_stats2(o_):
        for tt in range(2):
            it = o_ * 2 + tt
            sl = (it % 2) * 512
            p.op("scalar", (lambda o2, tt2, sl2: lambda e: e.copy(
                out=basebf[:, sl2: sl2 + 512],
                in_=ACC[:, o2 * 1024 + tt2 * 512: o2 * 1024 + tt2 * 512 + 512]))(o_, tt, sl),
                waits=[(VE, dn_evac[(NCHUNK - 1, o_, tt)]), (PE, stat_pe2.get(it - 2, 0)), (G, A_g)], inc=SC)
            s2 = p.op("scalar", (lambda o2, tt2, sl2: lambda e: e.activation(
                out=gbf[:, sl2: sl2 + 512],
                in_=ACC[:, o2 * 1024 + tt2 * 512: o2 * 1024 + tt2 * 512 + 512],
                func=AF.Square))(o_, tt, sl), inc=SC)
            p.op("tensor", (lambda tt2, o2, sl2: lambda e: e.matmul(
                psG[0:1, tt2 * 512:(tt2 + 1) * 512], onesbf[:], basebf[:, sl2: sl2 + 512],
                start=(o2 == 0), stop=(o2 == NO - 1)))(tt, o_, sl), waits=[(SC, s2)], inc=PE)
            p.op("tensor", (lambda tt2, o2, sl2: lambda e: e.matmul(
                psC[0:1, tt2 * 512:(tt2 + 1) * 512], onesbf[:], gbf[:, sl2: sl2 + 512],
                start=(o2 == 0), stop=(o2 == NO - 1)))(tt, o_, sl), inc=PE)
            stat_pe2[it] = p.val(PE)

    emit_job(0)
    for k, job in enumerate(jobs):
        csl = (k % 2) * 2
        if job[0] == "gu":
            _, ib, vg, vu = job
            pair = ib % 2
            psg_ = psA if pair == 0 else psC
            psu_ = psG if pair == 0 else psD
            hcol = (ib - 1000) if ib >= 1000 else (ib % IBC)
            extra_h = ib >= 1000
            prev = gu_prev[pair]
            gw_ = [(VE, castv[k]), (SC, silu_done.get(prev, 0))] + (first_h_wait if ib == 0 else [])
            uw_ = [(VE, h_done.get(prev, 0)), (VE, psDfree[0] if pair == 1 else 0)]
            if ib == 1:
                gw_.append((VE, v_cbc))
                uw_.append((VE, v_cbc))
            for kk in range(NK):
                for tt in range(2):
                    p.op("tensor", (lambda k_, tt_, sl, pg: lambda e: e.matmul(
                        pg[:, tt_ * 512:(tt_ + 1) * 512],
                        wcast[:, sl * 2048 + k_ * 128: sl * 2048 + (k_ + 1) * 128],
                        actb[:, k_ * 1024 + tt_ * 512: k_ * 1024 + (tt_ + 1) * 512],
                        start=(k_ == 0), stop=(k_ == NK - 1)))(kk, tt, csl, psg_),
                        waits=gw_ if (kk == 0 and tt == 0) else (), inc=PE if (kk == NK - 1 and tt == 1) else None)
            for kk in range(NK):
                for tt in range(2):
                    p.op("tensor", (lambda k_, tt_, sl, pu: lambda e: e.matmul(
                        pu[:, tt_ * 512:(tt_ + 1) * 512],
                        wcast[:, (sl + 1) * 2048 + k_ * 128: (sl + 1) * 2048 + (k_ + 1) * 128],
                        actb[:, k_ * 1024 + tt_ * 512: k_ * 1024 + (tt_ + 1) * 512],
                        start=(k_ == 0), stop=(k_ == NK - 1)))(kk, tt, csl, psu_),
                        waits=uw_ if (kk == 0 and tt == 0) else (), inc=PE if (kk == NK - 1 and tt == 1) else None)
            pe_done[k] = p.val(PE)
            emit_job(k + 1)
            ssl = p.op("scalar", (lambda pg: lambda e: e.activation(
                out=strips[:, 0:1024], in_=pg[:, 0:1024], func=AF.Silu))(psg_),
                waits=[(PE, pe_done[k])], inc=SC)
            silu_done[ib] = ssl
            if not extra_h:
                for tt in range(2):
                    vh = p.op("vector", (lambda hc, tt_, pu: lambda e: e.tensor_tensor(
                        out=HH[:, hc * 1024 + tt_ * 512: hc * 1024 + (tt_ + 1) * 512],
                        in0=strips[:, tt_ * 512:(tt_ + 1) * 512],
                        in1=pu[:, tt_ * 512:(tt_ + 1) * 512],
                        op=ALU.mult))(hcol, tt, psu_), waits=[(SC, ssl)] if tt == 0 else (), inc=VE)
            else:
                for tt in range(2):
                    p.op("vector", (lambda tt_, pu: lambda e: e.tensor_tensor(
                        out=strips[:, 1024 + tt_ * 512: 1024 + (tt_ + 1) * 512],
                        in0=strips[:, tt_ * 512:(tt_ + 1) * 512],
                        in1=pu[:, tt_ * 512:(tt_ + 1) * 512], op=ALU.mult))(tt, psu_),
                        waits=[(SC, ssl)] if tt == 0 else (), inc=VE)
                vh = p.op("vector", (lambda hc: lambda e: e.tensor_tensor(
                    out=HH2[:, hc * 1024:(hc + 1) * 1024], in0=strips[:, 1024:2048], in1=CBC,
                    op=ALU.mult))(hcol), inc=VE)
            h_done[ib] = vh
            gu_prev[pair] = ib
        else:
            _, c, o = job
            extra_routed = (c == NCHUNK - 1)
            h_last = h_done[(1000 + NRIB - 1)] if extra_routed else h_done[c * IBC + IBC - 1]
            if c == 0:
                vxr = p.dma("gpsimd", (lambda o_: lambda e: e.dma_start(
                    out=strips[:, 1024:2048],
                    in_=dx2[o_ * 128:(o_ + 1) * 128, 0:1024]))(o),
                    waits=[(G, v_x2g), (VE, dn_evac.get((c, o - 1, 1), 0))], inc=G)
            for tt in range(2):
                mw_ = [(VE, castv[k]), (VE, h_last), (VE, dn_evac.get((c, o - 1, tt), 0))]
                for kk in range(16):
                    p.op("tensor", (lambda kk_, tt_, sl: lambda e: e.matmul(
                        psD[:, tt_ * 512:(tt_ + 1) * 512],
                        wcast[:, sl * 2048 + kk_ * 128: sl * 2048 + (kk_ + 1) * 128],
                        HH[:, kk_ * 1024 + tt_ * 512: kk_ * 1024 + (tt_ + 1) * 512],
                        start=(kk_ == 0), stop=(kk_ == 15 and not extra_routed)))(kk, tt, csl),
                        waits=mw_ if kk == 0 else (), inc=PE if (kk == 15 and not extra_routed) else None)
                if extra_routed:
                    for kk in range(8):
                        p.op("tensor", (lambda kk_, tt_, sl: lambda e: e.matmul(
                            psD[:, tt_ * 512:(tt_ + 1) * 512],
                            wcast[:, (sl + 1) * 2048 + kk_ * 128: (sl + 1) * 2048 + (kk_ + 1) * 128],
                            HH2[:, kk_ * 1024 + tt_ * 512: kk_ * 1024 + (tt_ + 1) * 512],
                            start=False, stop=(kk_ == 7)))(kk, tt, csl), inc=PE if kk == 7 else None)
            pe_done[k] = p.val(PE)
            emit_job(k + 1)
            for tt in range(2):
                if c == 0:
                    vde = p.op("vector", (lambda o_, tt_: lambda e: e.tensor_tensor(
                        out=ACC[:, o_ * 1024 + tt_ * 512: o_ * 1024 + tt_ * 512 + 512],
                        in0=psD[:, tt_ * 512:(tt_ + 1) * 512],
                        in1=strips[:, 1024 + tt_ * 512: 1024 + (tt_ + 1) * 512], op=ALU.add))(o, tt),
                        waits=[(PE, pe_done[k]), (G, vxr)], inc=VE)
                else:
                    vde = p.op("vector", (lambda o_, tt_: lambda e: e.tensor_tensor(
                        out=ACC[:, o_ * 1024 + tt_ * 512: o_ * 1024 + tt_ * 512 + 512],
                        in0=psD[:, tt_ * 512:(tt_ + 1) * 512],
                        in1=ACC[:, o_ * 1024 + tt_ * 512: o_ * 1024 + tt_ * 512 + 512], op=ALU.add))(o, tt),
                        waits=[(PE, pe_done[k])], inc=VE)
                dn_evac[(c, o, tt)] = vde
            if c == NCHUNK - 1:
                if o >= 1:
                    emit_stats2(o - 1)
                if o == NO - 1:
                    emit_stats2(o)
            if o == NO - 1:
                psDfree[0] = p.val(VE)

    v_st2 = stat_pe2[2 * NO - 1]
    p.op("vector", lambda e: e.tensor_scalar_mul(out=mrow, in0=psG[0:1, 0:1024], scalar1=1.0 / D),
         waits=[(PE, v_st2)], inc=VE)
    p.op("vector", lambda e: e.tensor_scalar_mul(out=rstdrow, in0=psC[0:1, 0:1024], scalar1=1.0 / D), inc=VE)
    p.op("vector", lambda e: e.tensor_tensor(out=tmprow, in0=mrow, in1=mrow, op=ALU.mult), inc=VE)
    vvar2 = p.op("vector", lambda e: e.tensor_tensor(out=rstdrow, in0=rstdrow, in1=tmprow, op=ALU.subtract), inc=VE)
    vve2 = p.op("vector", lambda e: e.tensor_scalar(out=tmprow, in0=rstdrow, scalar1=EPS, scalar2=None, op0=ALU.add),
                waits=[(VE, vvar2)], inc=VE)
    vsq2 = p.op("scalar", lambda e: e.activation(out=rstdrow, in_=tmprow, func=AF.Sqrt),
                waits=[(VE, vve2)], inc=SC)
    vrs2 = p.op("vector", lambda e: e.reciprocal(out=tmprow, in_=rstdrow), waits=[(SC, vsq2)], inc=VE)
    for tt in range(2):
        p.op("tensor", (lambda tt_: lambda e: e.matmul(
            psD[:, tt_ * 512:(tt_ + 1) * 512], ones1f[:], mrow[0:1, tt_ * 512:(tt_ + 1) * 512],
            start=True, stop=True))(tt),
            waits=[(VE, vrs2), (PE, v_st2)] if tt == 0 else (), inc=PE)
        p.op("tensor", (lambda tt_: lambda e: e.matmul(
            psA[:, tt_ * 512:(tt_ + 1) * 512], ones1f[:], tmprow[0:1, tt_ * 512:(tt_ + 1) * 512],
            start=True, stop=True))(tt), inc=PE)
    v_bc2 = p.val(PE)
    out_prev = {}
    castv_last = castv[len(jobs) - 1]
    out_q = ["gpsimd", "sync", "scalar"]
    for o in range(NO):
        for tt in range(2):
            it = o * 2 + tt
            q = it % 8
            w_ = [(PE, v_bc2)]
            if it < 8:
                w_.append((VE, castv_last))
            else:
                w_.append((OUT, out_prev[it - 8]))
            vu = p.op("vector", (lambda o_, tt_, q_: lambda e: e.tensor_tensor(
                out=wstage[:, q_ * 512: q_ * 512 + 512],
                in0=ACC[:, o_ * 1024 + tt_ * 512: o_ * 1024 + tt_ * 512 + 512],
                in1=psD[:, tt_ * 512:(tt_ + 1) * 512], op=ALU.subtract))(o, tt, q), waits=w_, inc=VE)
            vo = p.op("vector", (lambda tt_, q_: lambda e: e.tensor_tensor(
                out=wstage[:, q_ * 512: q_ * 512 + 512], in0=wstage[:, q_ * 512: q_ * 512 + 512],
                in1=psA[:, tt_ * 512:(tt_ + 1) * 512], op=ALU.mult))(tt, q), inc=VE)
            vd = p.dma(out_q[it % 3], (lambda o_, tt_, q_: lambda e: e.dma_start(
                out=dout[o_ * 128:(o_ + 1) * 128, tt_ * 512:(tt_ + 1) * 512],
                in_=wstage[:, q_ * 512: q_ * 512 + 512]))(o, tt, q), waits=[(VE, vo)], inc=OUT)
            out_prev[it] = p.val(OUT)
    v_out = p.val(OUT)
    if dbg:
        for (dst0, dst1, src) in ((0, 32, CARRY), (32, 96, EFSB), (96, 104, COMB8),
                                  (232, 248, SCR), (608, 672, LOGIT), (672, 1696, CBC)):
            p.dma("gpsimd", (lambda d0, d1, s_: lambda e: e.dma_start(
                out=ddbg[:, d0:d1], in_=s_))(dst0, dst1, src), waits=[(OUT, v_out)], inc=OUT)
        p.dma("gpsimd", lambda e: e.dma_start(out=ddbg[0:8, 104:232], in_=comb8T[:]), inc=OUT)
        v_out = p.val(OUT)
    p.op("gpsimd", None, waits=[(OUT, v_out)])

    sems = {}
    for ch in (W, G, PE, SC, VE, CC, OUT):
        sems[ch] = es.enter_context(nc.semaphore(ch))

    with nc.Block() as block:
        def emit_for(name):
            hwm = {}

            def go(eng):
                for (e, fn, waits, inc, k) in p.ops:
                    if e != name:
                        continue
                    for (ch, v) in waits:
                        if hwm.get(ch, -1) < v:
                            eng.wait_ge(sems[ch], v)
                            hwm[ch] = v
                    ins = fn(eng) if fn is not None else None
                    if inc is not None:
                        if ins is None:
                            ins = eng.nop()
                        ins.then_inc(sems[inc], k)
            return go

        block.sync(emit_for("sync"))
        block.scalar(emit_for("scalar"))
        block.tensor(emit_for("tensor"))
        block.vector(emit_for("vector"))
        block.gpsimd(emit_for("gpsimd"))

    es.close()
    return nc



def _hostprep(inputs):
    f32 = np.float32
    x = np.asarray(inputs["L0_attn_hidden_states"], f32)
    dt_w = np.asarray(inputs["L0_attn_dt_w"], f32)
    conv_w = np.asarray(inputs["L0_attn_conv_w"], f32)

    shared = {
        "winT": np.ascontiguousarray(np.asarray(inputs["L0_attn_in_proj_w"], f32).T),
        "wgT": np.ascontiguousarray(np.asarray(inputs["L0_attn_gate_proj_w"], f32).T),
        "woT": np.ascontiguousarray(np.asarray(inputs["L0_attn_out_proj_w"], f32).T),
        "wsgT": np.ascontiguousarray(np.asarray(inputs["L0_ffn_sh_gate"], f32).T),
        "wsuT": np.ascontiguousarray(np.asarray(inputs["L0_ffn_sh_up"], f32).T),
        "wsdT": np.ascontiguousarray(np.asarray(inputs["L0_ffn_sh_down"], f32).T),
        "wrgT": np.ascontiguousarray(np.asarray(inputs["L0_ffn_r_gate"], f32).T),
        "wruT": np.ascontiguousarray(np.asarray(inputs["L0_ffn_r_up"], f32).T),
        "wrdT": np.ascontiguousarray(np.asarray(inputs["L0_ffn_r_down"], f32).T),
        "wrtT": np.ascontiguousarray(np.asarray(inputs["L0_ffn_router_weight"], f32).T),
        "ident": np.eye(128, dtype=f32),
    }
    sdt = np.zeros((NCB, 128, 128), f32)
    dtw_r = dt_w.reshape(NCB, 2, 64)
    for h2 in range(2):
        sdt[:, h2 * 64:(h2 + 1) * 64, h2 * 64:(h2 + 1) * 64] = dtw_r[:, h2][:, :, None]
    shared["sdt"] = np.ascontiguousarray(sdt.transpose(1, 0, 2).reshape(128, NCB * 128))
    z = np.zeros((NCB, 4, 128, 128), f32)
    cw_r = conv_w.reshape(NCB, 128, 4)
    z[:, :, np.arange(128), np.arange(128)] = cw_r.transpose(0, 2, 1)[:, :, :]
    shared["diag"] = np.ascontiguousarray(z.transpose(2, 0, 1, 3).reshape(128, NCB * 4 * 128))

    def chancols(v):
        return np.ascontiguousarray(np.asarray(v, f32).reshape(-1).reshape(NCB, 128).T)

    shared["alog"] = chancols(inputs["L0_attn_A_log"])
    shared["bp"] = chancols(inputs["L0_attn_Bp"])
    shared["cp"] = chancols(inputs["L0_attn_Cp"])
    shared["dp"] = chancols(inputs["L0_attn_Dp"])
    shared["convb"] = chancols(inputs["L0_attn_conv_b"])
    shared["dtbc"] = np.ascontiguousarray(
        np.repeat(np.asarray(inputs["L0_attn_dt_b"], f32), 64).reshape(NCB, 128).T)

    in_maps = []
    for c in range(8):
        b, q = c // 4, c % 4
        xs = np.zeros((D, TC), f32)
        if q == 0:
            xs[:, HALO:] = x[b, 0:T, :].T
        else:
            xs[:, :] = x[b, q * T - HALO:(q + 1) * T, :].T
        cm = np.zeros((128, 8), f32)
        for j in range(8):
            if j // 4 == b and (j % 4) < q:
                cm[:, j] = 1.0
        m = dict(shared)
        m["x"] = np.ascontiguousarray(xs)
        m["cmask"] = cm
        in_maps.append(m)
    return in_maps


DBG_MODE = False


def kernel(**inputs):
    in_maps = _hostprep(inputs)
    nc = build_nc(dbg=DBG_MODE)
    res = run_bass_kernel_spmd(nc, in_maps, core_ids=list(range(8)))
    out = np.empty((B, L, D), np.float32)
    for c in range(8):
        b, q = c // 4, c % 4
        out[b, q * T:(q + 1) * T, :] = res.results[c]["o"].T
    if DBG_MODE:
        return out, res.results
    return out


# version 16
# speedup vs baseline: 1.0021x; 1.0021x over previous
import numpy as np
import ml_dtypes
from contextlib import ExitStack

import concourse.bass as bass
import concourse.mybir as mybir
from concourse.bass_utils import run_bass_kernel_spmd

F32 = mybir.dt.float32
BF16 = mybir.dt.bfloat16
AF = mybir.ActivationFunctionType
ALU = mybir.AluOpType
AX = mybir.AxisListType.X
bf16 = ml_dtypes.bfloat16

B, L, D = 2, 4096, 2048
INNER = 4096
T = 1024
HALO = 4
TC = T + HALO
NK = D // 128
NCB = INNER // 128
NO = D // 128
FI = 8192
NIB = FI // 128
NCHUNK = 4
IBC = NIB // NCHUNK
RI = 1024
NRIB = RI // 128
NEG = -1.0e30
EPS = 1e-5


class Prog:

    def __init__(self):
        self.ops = []
        self.cnt = {}

    def val(self, ch):
        return self.cnt.get(ch, 0)

    def op(self, eng, fn=None, waits=(), inc=None, k=1):
        w = tuple((c, v) for (c, v) in waits if v and v > 0)
        self.ops.append((eng, fn, w, inc, k))
        if inc is not None:
            self.cnt[inc] = self.cnt.get(inc, 0) + k
            return self.cnt[inc]
        return None

    def dma(self, eng, fn, waits=(), inc=None):
        return self.op(eng, fn, waits, inc, 16)


_ACT_FALLBACK = {"silu_and_others": 18, "natural_log_exp_and_others": 6, "sqrt_and_others": 3}


def _act_set_id(name):
    try:
        from concourse.hw_specs import get_activation_tables
        keys = list(get_activation_tables("TRN2").keys())
        return keys.index(name)
    except Exception:
        pass
    try:
        import json as _json
        import os as _os
        import neuronxcc
        p = _os.path.join(_os.path.dirname(neuronxcc.__file__), "pwp", "pwp_bin_trainium", "act_info.json")
        sets = _json.load(open(p))["act_func_sets"]
        return [s["name"] for s in sets].index(name)
    except Exception:
        return _ACT_FALLBACK[name]


def build_nc(dbg=False):
    nc = bass.Bass("TRN2", target_bir_lowering=False, debug=False)
    ikind = {"kind": "ExternalOutput"} if dbg else {}

    dx = nc.dram_tensor("x", [D, TC], F32, kind="ExternalInput").ap()
    dwin = nc.dram_tensor("winT", [D, INNER], F32, kind="ExternalInput").ap()
    dwg = nc.dram_tensor("wgT", [D, INNER], F32, kind="ExternalInput").ap()
    dwo = nc.dram_tensor("woT", [INNER, D], F32, kind="ExternalInput").ap()
    dwsg = nc.dram_tensor("wsgT", [D, FI], F32, kind="ExternalInput").ap()
    dwsu = nc.dram_tensor("wsuT", [D, FI], F32, kind="ExternalInput").ap()
    dwsd = nc.dram_tensor("wsdT", [FI, D], F32, kind="ExternalInput").ap()
    dwrg = nc.dram_tensor("wrgT", [D, RI], F32, kind="ExternalInput").ap()
    dwru = nc.dram_tensor("wruT", [D, RI], F32, kind="ExternalInput").ap()
    dwrd = nc.dram_tensor("wrdT", [RI, D], F32, kind="ExternalInput").ap()
    dwrt = nc.dram_tensor("wrtT", [D, 8], F32, kind="ExternalInput").ap()
    dsdt = nc.dram_tensor("sdt", [128, NCB * 128], F32, kind="ExternalInput").ap()
    ddiag = nc.dram_tensor("diag", [128, NCB * 4 * 128], F32, kind="ExternalInput").ap()
    dalog = nc.dram_tensor("alog", [128, NCB], F32, kind="ExternalInput").ap()
    dbp = nc.dram_tensor("bp", [128, NCB], F32, kind="ExternalInput").ap()
    dcp = nc.dram_tensor("cp", [128, NCB], F32, kind="ExternalInput").ap()
    ddp = nc.dram_tensor("dp", [128, NCB], F32, kind="ExternalInput").ap()
    dconvb = nc.dram_tensor("convb", [128, NCB], F32, kind="ExternalInput").ap()
    ddtb = nc.dram_tensor("dtbc", [128, NCB], F32, kind="ExternalInput").ap()
    dcmask = nc.dram_tensor("cmask", [128, 8], F32, kind="ExternalInput").ap()
    dident = nc.dram_tensor("ident", [128, 128], F32, kind="ExternalInput").ap()
    dout = nc.dram_tensor("o", [D, T], F32, kind="ExternalOutput").ap()

    dspB = nc.dram_tensor("spillB", [2, NCB, 128, 512], BF16, **ikind)
    dspG = nc.dram_tensor("spillG", [2, NCB, 128, 512], BF16, **ikind)
    dx2 = nc.dram_tensor("x2sp", [D, T], F32, **ikind)
    defin = nc.dram_tensor("efin", [128, 64], F32)
    defout = nc.dram_tensor("efout", [8, 128, 64], F32)
    ddbg = nc.dram_tensor("dbg", [128, 8192], F32, kind="ExternalOutput").ap() if dbg else None
    ddbg2 = nc.dram_tensor("dbg2", [128, 2048], BF16, kind="ExternalOutput").ap() if dbg else None

    vwin = dwin.rearrange("(k p) c -> p k c", p=128)
    vwg = dwg.rearrange("(k p) c -> p k c", p=128)
    vwo = dwo.rearrange("(k p) c -> p k c", p=128)
    vwsg = dwsg.rearrange("(k p) c -> p k c", p=128)
    vwsu = dwsu.rearrange("(k p) c -> p k c", p=128)
    vwsd = dwsd.rearrange("(k p) c -> p k c", p=128)
    vwrg = dwrg.rearrange("(k p) c -> p k c", p=128)
    vwru = dwru.rearrange("(k p) c -> p k c", p=128)
    vwrd = dwrd.rearrange("(k p) c -> p k c", p=128)
    vwrt = dwrt.rearrange("(k p) c -> p k c", p=128)
    vx = dx.rearrange("(k p) t -> p k t", p=128)

    es = ExitStack()
    sb = lambda name, shape, dt: es.enter_context(nc.sbuf_tensor(name, shape, dt))
    psb = lambda name, shape: es.enter_context(nc.psum_tensor(name, shape, F32))

    actb = sb("actb", [128, NK * TC], BF16)
    bigb = sb("bigb", [128, 24640], BF16)
    bigf = sb("bigf", [128, 16384], F32)
    wstage = sb("wstage", [128, 4096], F32)
    wcast = sb("wcast", [128, 8192], BF16)
    strips = sb("strips", [128, 2048], F32)
    smallf = sb("smallf", [128, 1952], F32)
    identf = sb("identf", [128, 128], F32)
    comb8T = sb("comb8T", [8, 128], F32)
    ones1f = sb("ones1f", [1, 128], F32)
    onesbf = sb("onesbf", [128, 1], BF16)
    wrtb = sb("wrtb", [128, 128], BF16)
    rows = sb("rows", [1, 2048], F32)
    basebf = sb("basebf", [128, 1024], BF16)
    gbf = sb("gbf", [128, 1024], BF16)

    psA = psb("psA", [128, 1024])
    psG = psb("psG", [128, 1024])
    psC = psb("psC", [128, 1024])
    psD = psb("psD", [128, 1024])

    ACOL = smallf[:, 0:32]
    BPC = smallf[:, 32:64]
    CPH = smallf[:, 64:96]
    DPH = smallf[:, 96:128]
    CVB = smallf[:, 128:160]
    DTB = smallf[:, 160:192]
    CMASK = smallf[:, 192:200]
    ONESA = smallf[:, 200:232]
    CARRY = smallf[:, 232:264]
    LOGIT = smallf[:, 264:328]
    COMB8 = smallf[:, 328:336]
    EFSB = smallf[:, 336:400]
    GATH = smallf[:, 400:912]
    CBC = smallf[:, 912:1936]
    SCR = smallf[:, 1936:1952]

    ET = bigf[:, 10240:11264]
    EP1 = bigf[:, 11264:12288]
    SLOC = bigf[:, 12288:13312]
    PP = bigf[:, 13312:14336]
    DRV = bigf[:, 14336:15360]
    Y1 = bigf[:, 0:16384]
    ACC = bigf[:, 0:16384]

    DIAGB = bigb[:, 0:16384]
    SDTB = bigb[:, 16384:20480]
    PROJB = bigb[:, 20480:21508]
    XB = bigb[:, 21536:23584]
    COMBINED = bigb[:, 0:16384]
    BLB8 = bigb[:, 16384:20480]
    BLG8 = bigb[:, 20480:24576]
    STATB = bigb[:, 21536:22560]
    HH = bigb[:, 0:16384]
    HH2 = bigb[:, 16384:24576]

    ZEROS = strips[:, 0:1024]

    SET_SILU = _act_set_id("silu_and_others")
    SET_NLEXP = _act_set_id("natural_log_exp_and_others")
    SET_SQRT = _act_set_id("sqrt_and_others")

    def actload(set_id):
        def fn(s):
            return s.add_instruction(mybir.InstLoadActFuncSet(
                name=nc.get_next_instruction_name(), act_func_set_id=set_id,
                ins=[], outs=[]))
        return fn

    p = Prog()
    W, G, PE, SC, VE, CC, OUT = "wld", "gld", "pe", "sc", "ve", "cc", "out"

    p.op("gpsimd", lambda e: e.collective_compute(
        "AllGather", ALU.bypass, replica_groups=[list(range(8))],
        ins=[defin.ap()], outs=[defout.ap()]), inc=CC)
    for ap_dst, ap_src in ((smallf[:, 0:32], dalog), (BPC, dbp), (smallf[:, 64:96], dcp),
                           (smallf[:, 96:128], ddp), (CVB, dconvb), (DTB, ddtb),
                           (CMASK, dcmask), (identf[:], dident)):
        p.dma("sync", (lambda d, s: lambda e: e.dma_start(out=d, in_=s))(ap_dst, ap_src), inc=W)
    v_small = p.val(W)
    p.dma("sync", lambda e: e.dma_start(out=wstage[:, 0:128], in_=vwrt), inc=W)
    v_wrt = p.val(W)
    p.op("scalar", lambda e: e.copy(out=wrtb[:], in_=wstage[:, 0:128]), waits=[(W, v_wrt)], inc=SC)
    p.op("vector", lambda e: e.memset(ONESA, 1.0), inc=VE)
    p.op("vector", lambda e: e.memset(onesbf[:], 1.0), inc=VE)
    p.op("vector", lambda e: e.memset(ones1f[:], 1.0), inc=VE)
    p.op("vector", lambda e: e.memset(ZEROS, 0.0), inc=VE)
    p.op("scalar", lambda e: e.activation(out=LOGIT[:, 0:32], in_=smallf[:, 0:32], func=AF.Exp),
         waits=[(W, v_small)], inc=SC)
    v_sc0 = p.val(SC)
    p.op("vector", lambda e: e.tensor_scalar_mul(out=ACOL, in0=LOGIT[:, 0:32], scalar1=-1.0),
         waits=[(SC, v_sc0)], inc=VE)
    p.op("vector", lambda e: e.tensor_scalar_mul(out=CPH, in0=smallf[:, 64:96], scalar1=0.5),
         waits=[(W, v_small)], inc=VE)
    p.op("vector", lambda e: e.tensor_scalar_mul(out=DPH, in0=smallf[:, 96:128], scalar1=0.5),
         waits=[(W, v_small)], inc=VE)
    v_const = p.val(VE)

    def w_slice(view, cb_, width=128):
        return view[:, :, cb_ * width:(cb_ + 1) * width]

    stage_jobs = []
    for k in range(NK):
        stage_jobs.append((vx[:, k, :], TC, actb[:, k * TC:(k + 1) * TC]))
    for j in range(8):
        stage_jobs.append((ddiag[:, j * 2048:(j + 1) * 2048], 2048, DIAGB[:, j * 2048:(j + 1) * 2048]))
    for j in range(2):
        stage_jobs.append((dsdt[:, j * 2048:(j + 1) * 2048], 2048, SDTB[:, j * 2048:(j + 1) * 2048]))
    wrt_cast = p.val(SC)
    qrr = ["sync", "gpsimd", "scalar"]
    cast_done = {}
    for ci, (src, w_, dst) in enumerate(stage_jobs):
        slot = ci % 2
        wait_cast = cast_done[ci - 2] if ci >= 2 else (SC, wrt_cast)
        v = p.dma(qrr[ci % 3], (lambda s_, w2, sl: lambda e: e.dma_start(out=wstage[:, sl * 2048: sl * 2048 + w2], in_=s_))(src, w_, slot),
                  waits=[wait_cast], inc=W)
        if ci % 2 == 0:
            cast_done[ci] = (SC, p.op("scalar", (lambda d_, w2, sl: lambda e: e.copy(out=d_, in_=wstage[:, sl * 2048: sl * 2048 + w2]))(dst, w_, slot),
                                      waits=[(W, v)], inc=SC))
        else:
            cast_done[ci] = (VE, p.op("vector", (lambda d_, w2, sl: lambda e: e.tensor_copy(out=d_, in_=wstage[:, sl * 2048: sl * 2048 + w2]))(dst, w_, slot),
                                      waits=[(W, v)], inc=VE))
    nsj = len(stage_jobs)
    v_setup_ve = p.val(VE)
    sc_x = p.val(SC)
    ve_x = v_setup_ve
    ve_diag = v_setup_ve
    vv = p.dma("sync", (lambda: lambda e: e.dma_start(out=wstage[:, 0:2048], in_=w_slice(vwin, 0)))(),
               waits=[cast_done[nsj - 2]], inc=W)
    vv2 = p.dma("sync", (lambda: lambda e: e.dma_start(out=wstage[:, 2048:4096], in_=w_slice(vwg, 0)))(),
                waits=[cast_done[nsj - 1]], inc=W)
    c_win = p.op("scalar", lambda e: e.copy(out=wcast[:, 0:2048], in_=wstage[:, 0:2048]), waits=[(W, vv)], inc=SC)
    c_wg = p.op("scalar", lambda e: e.copy(out=wcast[:, 2048:4096], in_=wstage[:, 2048:4096]), waits=[(W, vv2)], inc=SC)

    A = {}

    def rec(cb_, key, v):
        A[(cb_, key)] = v

    def get(cb_, key, default=0):
        return A.get((cb_, key), default)

    def w_slice2(view, cb_):
        return view[:, :, cb_ * 128:(cb_ + 1) * 128]

    rec(0, "cast_w", (SC, c_win))
    rec(0, "cast_g", (SC, c_wg))
    if NCB > 1:
        vs1 = p.dma("sync", lambda e: e.dma_start(out=wstage[:, 0:2048], in_=w_slice2(vwin, 1)),
                    waits=[(SC, c_wg)], inc=W)
        vs2 = p.dma("sync", lambda e: e.dma_start(out=wstage[:, 2048:4096], in_=w_slice2(vwg, 1)), inc=W)
        rec(1, "staged", (vs1, vs2))

    for i in range(NCB + 2):
        j1 = i - 1
        j2 = i - 2
        par_i = i % 2
        if i + 1 <= NCB - 1 and (i + 1, "staged") in A and (i + 1, "cast_w") not in A:
            s1, s2 = A[(i + 1, "staged")]
            npar = (i + 1) % 2
            cwv = p.op("scalar", (lambda sl: lambda e: e.copy(
                out=wcast[:, (2 * sl) * 2048:(2 * sl) * 2048 + 2048], in_=wstage[:, 0:2048]))(npar),
                waits=[(W, s1), (PE, get(i - 1, "pe_gate"))], inc=SC)
            cgv = p.op("scalar", (lambda sl: lambda e: e.copy(
                out=wcast[:, (2 * sl + 1) * 2048:(2 * sl + 1) * 2048 + 2048], in_=wstage[:, 2048:4096]))(npar),
                waits=[(W, s2)], inc=SC)
            rec(i + 1, "cast_w", (SC, cwv))
            rec(i + 1, "cast_g", (SC, cgv))
        if i < NCB:
            winb = wcast[:, (2 * par_i) * 2048:(2 * par_i) * 2048 + 2048]
            wgb = wcast[:, (2 * par_i + 1) * 2048:(2 * par_i + 1) * 2048 + 2048]
            hw = [A[(i, "cast_w")], (SC, get(i - 3, "e")), (SC, get(i - 1, "evacp"))]
            if i == 0:
                hw.append((SC, sc_x))
                hw.append((VE, ve_x))
            for k in range(NK):
                p.op("tensor", (lambda k_, wb_: lambda e: e.matmul(
                    psD[:, 0:4], wb_[:, k_ * 128:(k_ + 1) * 128],
                    actb[:, k_ * TC: k_ * TC + HALO],
                    start=(k_ == 0), stop=(k_ == NK - 1)))(k, winb),
                    waits=hw if k == 0 else (), inc=PE if k == NK - 1 else None)
            for k in range(NK):
                p.op("tensor", (lambda k_, wb_: lambda e: e.matmul(
                    psA[:, 0:512], wb_[:, k_ * 128:(k_ + 1) * 128],
                    actb[:, k_ * TC + HALO: k_ * TC + HALO + 512],
                    start=(k_ == 0), stop=(k_ == NK - 1)))(k, winb), inc=PE if k == NK - 1 else None)
                p.op("tensor", (lambda k_, wb_: lambda e: e.matmul(
                    psA[:, 512:1024], wb_[:, k_ * 128:(k_ + 1) * 128],
                    actb[:, k_ * TC + HALO + 512: k_ * TC + TC],
                    start=(k_ == 0), stop=(k_ == NK - 1)))(k, winb), inc=PE if k == NK - 1 else None)
            rec(i, "pe_proj", p.val(PE))
            gw = [A[(i, "cast_g")], (SC, get(i - 1, "thg"))]
            for k in range(NK):
                p.op("tensor", (lambda k_, wb_: lambda e: e.matmul(
                    psG[:, 0:512], wb_[:, k_ * 128:(k_ + 1) * 128],
                    actb[:, k_ * TC + HALO: k_ * TC + HALO + 512],
                    start=(k_ == 0), stop=(k_ == NK - 1)))(k, wgb),
                    waits=gw if k == 0 else (), inc=PE if k == NK - 1 else None)
                p.op("tensor", (lambda k_, wb_: lambda e: e.matmul(
                    psG[:, 512:1024], wb_[:, k_ * 128:(k_ + 1) * 128],
                    actb[:, k_ * TC + HALO + 512: k_ * TC + TC],
                    start=(k_ == 0), stop=(k_ == NK - 1)))(k, wgb), inc=PE if k == NK - 1 else None)
            rec(i, "pe_gate", p.val(PE))
        if 0 <= j1 < NCB:
            cw = [(SC, get(j1, "evacp")), (SC, get(j1 - 1, "X"))]
            if j1 == 0:
                cw.append((VE, ve_diag))
            for tt in range(2):
                for k in range(4):
                    p.op("tensor", (lambda tt_, k_, cb_: lambda e: e.matmul(
                        psC[:, tt_ * 512:(tt_ + 1) * 512],
                        DIAGB[:, (cb_ * 4 + k_) * 128:(cb_ * 4 + k_ + 1) * 128],
                        PROJB[:, 1 + k_ + tt_ * 512: 513 + k_ + tt_ * 512],
                        start=(k_ == 0), stop=(k_ == 3)))(tt, k, j1),
                        waits=cw if (tt == 0 and k == 0) else (), inc=PE if k == 3 else None)
            rec(j1, "pe_conv", p.val(PE))
        if i < NCB:
            ew = [(PE, get(i, "pe_proj")), (PE, get(i - 1, "pe_conv"))]
            p.op("scalar", lambda e: e.copy(out=PROJB[:, 0:HALO], in_=psD[:, 0:4]), waits=ew, inc=SC)
            rec(i, "evacp", p.op("scalar", lambda e: e.copy(out=PROJB[:, HALO:TC], in_=psA[:, 0:1024]), inc=SC))
        if 0 <= j2 < NCB:
            dtw = [(SC, get(j2, "Xb")), (SC, get(j2 - 1, "e")), (SC, get(min(i, NCB - 1), "evacp"))]
            if j2 == 0:
                dtw.append((VE, v_setup_ve))
            for tt in range(2):
                p.op("tensor", (lambda tt_, cb_: lambda e: e.matmul(
                    psD[:, tt_ * 512:(tt_ + 1) * 512],
                    SDTB[:, cb_ * 128:(cb_ + 1) * 128],
                    XB[:, (cb_ % 2) * 1024 + tt_ * 512: (cb_ % 2) * 1024 + (tt_ + 1) * 512],
                    start=True, stop=True))(tt, j2),
                    waits=dtw if tt == 0 else (), inc=PE if tt == 1 else None)
            rec(j2, "pe_dt", p.val(PE))
        if 0 <= j2 < NCB:
            rec(j2, "e", p.op("scalar", (lambda cb_: lambda e: e.activation(
                out=ET, in_=psD[:, 0:1024], func=AF.Exp, bias=DTB[:, cb_:cb_ + 1], scale=1.0))(j2),
                waits=[(PE, get(j2, "pe_dt")), (VE, get(j2 - 1, "ve_basev"))], inc=SC))
            rec(j2, "ep1", p.op("vector", lambda e: e.tensor_scalar(
                out=EP1, in0=ET, scalar1=1.0, scalar2=None, op0=ALU.add),
                waits=[(SC, get(j2, "e"))], inc=VE))
            dtb_off = 6144 + (j2 % 2) * 1024
            dec_off = 8192 + (j2 % 2) * 1024
            p.op("scalar", (lambda off: lambda e: e.activation(
                out=bigf[:, off: off + 1024], in_=EP1, func=AF.Ln))(dtb_off),
                waits=[(VE, get(j2, "ep1")), (VE, get(j2 - 2, "ve_drive"))], inc=SC)
            rec(j2, "decay", p.op("scalar", (lambda doff, soff, cb_: lambda e: e.activation(
                out=bigf[:, doff: doff + 1024], in_=bigf[:, soff: soff + 1024],
                func=AF.Exp, scale=ACOL[:, cb_:cb_ + 1]))(dec_off, dtb_off, j2),
                waits=[(VE, get(j2 - 2, "ve_pp"))], inc=SC))
        if i < NCB:
            th_sl = i % 4
            thw = [(PE, get(i, "pe_gate")), (VE, get(i - 4, "ve_g"))]
            rec(i, "thg", p.op("scalar", (lambda sl: lambda e: e.activation(
                out=bigf[:, sl * 1024: sl * 1024 + 1024], in_=psG[:, 0:1024], func=AF.Tanh, scale=0.5))(th_sl),
                waits=thw, inc=SC))
        if 0 <= j1 < NCB:
            x_sl = 4096 + (j1 % 2) * 1024
            xw = [(PE, get(j1, "pe_conv")), (VE, get(j1 - 2, "ve_basev"))]
            rec(j1, "X", p.op("scalar", (lambda off, cb_: lambda e: e.activation(
                out=bigf[:, off: off + 1024], in_=psC[:, 0:1024],
                func=AF.Silu, bias=CVB[:, cb_:cb_ + 1], scale=1.0))(x_sl, j1), waits=xw, inc=SC))
            rec(j1, "Xb", p.op("scalar", (lambda off, cb_: lambda e: e.copy(
                out=XB[:, (cb_ % 2) * 1024:(cb_ % 2) * 1024 + 1024], in_=bigf[:, off: off + 1024]))(x_sl, j1),
                waits=[(PE, get(j1 - 2, "pe_dt"))], inc=SC))
        if i + 2 <= NCB - 1:
            ws = [A[(i + 1, "cast_g")] if (i + 1, "cast_g") in A else (SC, 0)]
            s1 = p.dma("sync", (lambda cb_: lambda e: e.dma_start(
                out=wstage[:, 0:2048], in_=w_slice2(vwin, cb_)))(i + 2), waits=ws, inc=W)
            s2 = p.dma("sync", (lambda cb_: lambda e: e.dma_start(
                out=wstage[:, 2048:4096], in_=w_slice2(vwg, cb_)))(i + 2), inc=W)
            rec(i + 2, "staged", (s1, s2))
        if 0 <= j2 < NCB:
            dtb_t = bigf[:, dtb_off: dtb_off + 1024]
            dec_t = bigf[:, dec_off: dec_off + 1024]
            x_t = bigf[:, 4096 + (j2 % 2) * 1024: 4096 + (j2 % 2) * 1024 + 1024]
            th_t = bigf[:, (j2 % 4) * 1024: (j2 % 4) * 1024 + 1024]
            rec(j2, "ve_drive", p.op("vector", (lambda dtb_, x_, cb_: lambda e: e.scalar_tensor_tensor(
                out=DRV, in0=dtb_, scalar=BPC[:, cb_:cb_ + 1], in1=x_, op0=ALU.mult, op1=ALU.mult))(dtb_t, x_t, j2),
                waits=[(SC, get(j2, "decay"))], inc=VE))
            p.op("vector", (lambda dec_: lambda e: e.tensor_tensor_scan(
                out=SLOC, data0=dec_, data1=DRV, initial=0.0, op0=ALU.mult, op1=ALU.add))(dec_t), inc=VE)
            rec(j2, "ve_pp", p.op("vector", (lambda dec_: lambda e: e.tensor_tensor_scan(
                out=PP, data0=dec_, data1=ZEROS, initial=1.0, op0=ALU.mult, op1=ALU.add))(dec_t), inc=VE))
            p.op("vector", (lambda cb_: lambda e: e.tensor_copy(out=EFSB[:, cb_:cb_ + 1], in_=PP[:, 1023:1024]))(j2), inc=VE)
            p.op("vector", (lambda cb_: lambda e: e.tensor_copy(out=EFSB[:, 32 + cb_:32 + cb_ + 1], in_=SLOC[:, 1023:1024]))(j2), inc=VE)
            p.op("vector", (lambda cb_: lambda e: e.tensor_scalar_mul(out=EP1, in0=SLOC, scalar1=CPH[:, cb_:cb_ + 1]))(j2), inc=VE)
            rec(j2, "ve_basev", p.op("vector", (lambda x_, cb_: lambda e: e.scalar_tensor_tensor(
                out=ET, in0=x_, scalar=DPH[:, cb_:cb_ + 1], in1=EP1, op0=ALU.mult, op1=ALU.add))(x_t, j2), inc=VE))
            rec(j2, "ve_base", p.op("vector", (lambda th_: lambda e: e.scalar_tensor_tensor(
                out=basebf[:], in0=th_, scalar=1.0, in1=ET, op0=ALU.add, op1=ALU.mult))(th_t),
                waits=[(G, get(j2 - 1, "spillB"))], inc=VE))
            p.op("vector", (lambda cb_: lambda e: e.tensor_scalar_mul(out=EP1, in0=PP, scalar1=CPH[:, cb_:cb_ + 1]))(j2), inc=VE)
            rec(j2, "ve_g", p.op("vector", (lambda th_: lambda e: e.scalar_tensor_tensor(
                out=gbf[:], in0=th_, scalar=1.0, in1=EP1, op0=ALU.add, op1=ALU.mult))(th_t),
                waits=[(G, get(j2 - 1, "spillG"))], inc=VE))
            for tt in range(2):
                p.dma("gpsimd", (lambda tt_, cb_: lambda e: e.dma_start(
                    out=dspB.ap()[tt_, cb_], in_=basebf[:, tt_ * 512:(tt_ + 1) * 512]))(tt, j2),
                    waits=[(VE, get(j2, "ve_base"))] if tt == 0 else (), inc=G)
            rec(j2, "spillB", p.val(G))
            for tt in range(2):
                p.dma("gpsimd", (lambda tt_, cb_: lambda e: e.dma_start(
                    out=dspG.ap()[tt_, cb_], in_=gbf[:, tt_ * 512:(tt_ + 1) * 512]))(tt, j2),
                    waits=[(VE, get(j2, "ve_g"))] if tt == 0 else (), inc=G)
            rec(j2, "spillG", p.val(G))

    A_pe = p.val(PE)
    A_sc = p.val(SC)
    A_ve = p.val(VE)
    if dbg:
        for (d0, src_) in ((1184, bigf[:, 9216:10240]), (2208, bigf[:, 7168:8192]),
                           (3232, bigf[:, 5120:6144]), (4256, bigf[:, 3072:4096]),
                           (7328, strips[:, 0:512]), (7840, smallf[:, 0:232])):
            w_ = src_.shape[1]
            p.dma("gpsimd", (lambda a, s_, w2: lambda e: e.dma_start(
                out=ddbg[:, a:a + w2], in_=s_))(d0, src_, w_), waits=[(VE, A_ve)], inc=G)
        p.dma("gpsimd", lambda e: e.dma_start(out=ddbg2[:, 0:1028], in_=PROJB), waits=[(VE, A_ve), (SC, A_sc)], inc=G)
    A_g = p.val(G)

    v = p.dma("gpsimd", lambda e: e.dma_start(out=defin.ap(), in_=EFSB), waits=[(VE, A_ve)], inc=G)
    cc1 = p.op("gpsimd", lambda e: e.collective_compute(
        "AllGather", ALU.bypass, replica_groups=[list(range(8))],
        ins=[defin.ap()], outs=[defout.ap()]), waits=[(G, v)], inc=CC)
    vg2 = p.dma("gpsimd", lambda e: e.dma_start(
        out=GATH, in_=defout.ap().rearrange("j p c -> p j c")), waits=[(CC, cc1)], inc=G)

    p.op("vector", lambda e: e.memset(CARRY, 0.0), waits=[(G, vg2)], inc=VE)
    for j in range(8):
        ej = GATH[:, j * 64: j * 64 + 32]
        fj = GATH[:, j * 64 + 32: j * 64 + 64]
        mj = CMASK[:, j:j + 1]
        p.op("vector", (lambda ej_: lambda e: e.tensor_scalar(
            out=ET[:, 0:32], in0=ej_, scalar1=-1.0, scalar2=None, op0=ALU.add))(ej), inc=VE)
        p.op("vector", lambda e: e.drain(), inc=VE)
        p.op("vector", (lambda mj_: lambda e: e.scalar_tensor_tensor(
            out=ET[:, 64:96], in0=ET[:, 0:32], scalar=mj_, in1=ONESA, op0=ALU.mult, op1=ALU.add))(mj), inc=VE)
        p.op("vector", (lambda fj_, mj_: lambda e: e.tensor_scalar_mul(
            out=ET[:, 32:64], in0=fj_, scalar1=mj_))(fj, mj), inc=VE)
        p.op("vector", lambda e: e.drain(), inc=VE)
        p.op("vector", lambda e: e.tensor_tensor(out=CARRY, in0=CARRY, in1=ET[:, 64:96], op=ALU.mult), inc=VE)
        p.op("vector", lambda e: e.drain(), inc=VE)
        p.op("vector", lambda e: e.tensor_tensor(out=CARRY, in0=CARRY, in1=ET[:, 32:64], op=ALU.add), inc=VE)
        p.op("vector", lambda e: e.drain(), inc=VE)
    p.op("vector", lambda e: e.tensor_copy(out=ET[:, 64:96], in_=CARRY), inc=VE)
    p.op("vector", lambda e: e.tensor_copy(out=ET[:, 64:96], in_=CARRY), inc=VE)
    v_carry = p.val(VE)

    mm_slot_done = {}
    evac_done = {}
    cast_done_c = {}
    stats_pe = 0
    wov = dwo.rearrange("(k p) c -> p k c", p=128)
    fix_done = {}
    stats_pe_h = [0]

    def emit_stats1(tt_e, o_e):
        p.op("scalar", (lambda o_, tt_: lambda e: e.copy(
            out=STATB[:, 0:512], in_=Y1[:, o_ * 1024 + tt_ * 512: o_ * 1024 + tt_ * 512 + 512]))(o_e, tt_e),
            waits=[(VE, evac_done[(tt_e, o_e)]), (PE, stats_pe_h[0])], inc=SC)
        sb2 = p.op("scalar", (lambda o_, tt_: lambda e: e.activation(
            out=STATB[:, 512:1024], in_=Y1[:, o_ * 1024 + tt_ * 512: o_ * 1024 + tt_ * 512 + 512],
            func=AF.Square))(o_e, tt_e), inc=SC)
        p.op("tensor", (lambda tt_, o_: lambda e: e.matmul(
            psG[0:1, tt_ * 512:(tt_ + 1) * 512], onesbf[:], STATB[:, 0:512],
            start=(o_ == 0), stop=(o_ == NO - 1)))(tt_e, o_e), waits=[(SC, sb2)], inc=PE)
        p.op("tensor", (lambda tt_, o_: lambda e: e.matmul(
            psC[0:1, tt_ * 512:(tt_ + 1) * 512], onesbf[:], STATB[:, 512:1024],
            start=(o_ == 0), stop=(o_ == NO - 1)))(tt_e, o_e), inc=PE)
        stats_pe_h[0] = p.val(PE)
        return stats_pe_h[0]

    wo_cast_prev = A[(NCB - 1, "cast_g")]
    for tt in range(2):
        for r4 in range(8):
            par4 = r4 % 2
            blb = bigb[:, 16384 + par4 * 2048: 16384 + par4 * 2048 + 2048]
            blg = bigb[:, 20480 + par4 * 2048: 20480 + par4 * 2048 + 2048]
            wb_ = [(PE, A_pe), (G, A_g)]
            if tt == 1:
                wb_.append((PE, stats_pe))
                wb_.append((VE, evac_done.get((0, 15), 0)))
            prevkey = (tt, r4 - 2) if r4 >= 2 else ((tt - 1, 6 + r4) if tt > 0 else None)
            if prevkey is not None:
                wb_.append((VE, fix_done.get(prevkey, 0)))
            vB = p.dma("gpsimd", (lambda tt_, r_, d_: lambda e: e.dma_start(
                out=d_, in_=dspB.ap()[tt_, r_ * 4:(r_ + 1) * 4].rearrange("b p t -> p b t")))(tt, r4, blb),
                waits=wb_, inc=G)
            vG2 = p.dma("gpsimd", (lambda tt_, r_, d_: lambda e: e.dma_start(
                out=d_, in_=dspG.ap()[tt_, r_ * 4:(r_ + 1) * 4].rearrange("b p t -> p b t")))(tt, r4, blg), inc=G)
            for i in range(4):
                cb = r4 * 4 + i
                p.op("vector", (lambda i_, cb_, bb_, gg_: lambda e: e.scalar_tensor_tensor(
                    out=COMBINED[:, cb_ * 512:(cb_ + 1) * 512],
                    in0=gg_[:, i_ * 512:(i_ + 1) * 512], scalar=CARRY[:, cb_:cb_ + 1],
                    in1=bb_[:, i_ * 512:(i_ + 1) * 512], op0=ALU.mult, op1=ALU.add))(i, cb, blb, blg),
                    waits=[(G, vG2), (VE, v_carry)] if i == 0 else (), inc=VE)
            fix_done[(tt, r4)] = p.val(VE)
        v_fix = p.val(VE)

        for o in range(NO):
            sw = [wo_cast_prev]
            vh0 = p.dma("sync", (lambda o_: lambda e: e.dma_start(
                out=wstage[:, 0:2048], in_=wov[:, 0:16, o_ * 128:(o_ + 1) * 128]))(o), waits=sw, inc=W)
            vh1 = p.dma("sync", (lambda o_: lambda e: e.dma_start(
                out=wstage[:, 2048:4096], in_=wov[:, 16:32, o_ * 128:(o_ + 1) * 128]))(o), inc=W)
            csl = (o % 2) * 2
            slot_prev = (tt, o - 2) if o >= 2 else (tt - 1, 14 + o)
            cw0 = p.op("vector", (lambda sl: lambda e: e.tensor_copy(
                out=wcast[:, sl * 2048:(sl + 1) * 2048], in_=wstage[:, 0:2048]))(csl),
                waits=[(W, vh0), (PE, mm_slot_done.get(slot_prev, A_pe))], inc=VE)
            cw1 = p.op("vector", (lambda sl: lambda e: e.tensor_copy(
                out=wcast[:, (sl + 1) * 2048:(sl + 2) * 2048], in_=wstage[:, 2048:4096]))(csl),
                waits=[(W, vh1)], inc=VE)
            wo_cast_prev = (VE, cw1)
            cast_done_c[o] = cw1
            vxs = p.dma("gpsimd", (lambda o_, tt_: lambda e: e.dma_start(
                out=strips[:, (o_ % 2) * 512:(o_ % 2) * 512 + 512],
                in_=dx[o_ * 128:(o_ + 1) * 128, HALO + tt_ * 512: HALO + (tt_ + 1) * 512]))(o, tt),
                waits=[(VE, evac_done.get((tt, o - 2), 0) if (tt, o - 2) in evac_done else (evac_done.get((tt - 1, 14 + o), 0)))], inc=G)
            bank = (o % 2) * 512
            mw = [(VE, cw1), (VE, evac_done.get((tt, o - 2), 0))]
            for kk in range(32):
                sl2 = csl * 2048 + (kk // 16) * 2048 + (kk % 16) * 128
                kw = list(mw) if kk == 0 else []
                if kk % 4 == 0:
                    kw.append((VE, fix_done[(tt, kk // 4)]))
                p.op("tensor", (lambda kk_, sl_, bank_, tt_: lambda e: e.matmul(
                    psA[:, bank_:bank_ + 512], wcast[:, sl_: sl_ + 128],
                    COMBINED[:, kk_ * 512:(kk_ + 1) * 512],
                    start=(kk_ == 0), stop=(kk_ == 31)))(kk, sl2, bank, tt),
                    waits=kw, inc=PE if kk == 31 else None)
            mm_slot_done[(tt, o)] = p.val(PE)
            vev = p.op("vector", (lambda o_, tt_, bank_: lambda e: e.tensor_tensor(
                out=Y1[:, o_ * 1024 + tt_ * 512: o_ * 1024 + tt_ * 512 + 512],
                in0=psA[:, bank_:bank_ + 512], in1=strips[:, (o_ % 2) * 512:(o_ % 2) * 512 + 512],
                op=ALU.add))(o, tt, bank),
                waits=[(PE, mm_slot_done[(tt, o)]), (G, vxs)], inc=VE)
            evac_done[(tt, o)] = vev
            if o >= 1:
                stats_pe = emit_stats1(tt, o - 1)
            if o == NO - 1:
                stats_pe = emit_stats1(tt, o)

    mrow = rows[0:1, 0:1024]
    rstdrow = rows[0:1, 1024:2048]
    tmprow = CBC[0:1, 0:1024]
    v1 = p.op("vector", lambda e: e.tensor_scalar_mul(out=mrow, in0=psG[0:1, 0:1024], scalar1=1.0 / D),
              waits=[(PE, stats_pe)], inc=VE)
    p.op("vector", lambda e: e.tensor_scalar_mul(out=rstdrow, in0=psC[0:1, 0:1024], scalar1=1.0 / D), inc=VE)
    p.op("vector", lambda e: e.tensor_tensor(out=tmprow, in0=mrow, in1=mrow, op=ALU.mult), inc=VE)
    vvar = p.op("vector", lambda e: e.tensor_tensor(out=rstdrow, in0=rstdrow, in1=tmprow, op=ALU.subtract), inc=VE)
    vve = p.op("vector", lambda e: e.tensor_scalar(out=tmprow, in0=rstdrow, scalar1=EPS, scalar2=None, op0=ALU.add),
               waits=[(VE, vvar)], inc=VE)
    vsq = p.op("scalar", lambda e: e.activation(out=rstdrow, in_=tmprow, func=AF.Sqrt),
               waits=[(VE, vve)], inc=SC)
    vrs = p.op("vector", lambda e: e.reciprocal(out=tmprow, in_=rstdrow), waits=[(SC, vsq)], inc=VE)
    bcw = [(VE, vrs), (PE, stats_pe)]
    for tt in range(2):
        p.op("tensor", (lambda tt_: lambda e: e.matmul(
            psD[:, tt_ * 512:(tt_ + 1) * 512], ones1f[:], mrow[0:1, tt_ * 512:(tt_ + 1) * 512],
            start=True, stop=True))(tt), waits=bcw if tt == 0 else (), inc=PE)
        p.op("tensor", (lambda tt_: lambda e: e.matmul(
            psA[:, tt_ * 512:(tt_ + 1) * 512], ones1f[:], tmprow[0:1, tt_ * 512:(tt_ + 1) * 512],
            start=True, stop=True))(tt), inc=PE)
    v_bc = p.val(PE)

    xsp_done = {}
    for o in range(NO):
        for tt in range(2):
            it = o * 2 + tt
            q = it % 4
            w_ = [(PE, v_bc)]
            if it >= 4:
                prev = xsp_done[it - 4]
                w_ += [(G, prev[0]), (SC, prev[1])]
            vu = p.op("vector", (lambda o_, tt_, q_: lambda e: e.tensor_tensor(
                out=strips[:, q_ * 512: q_ * 512 + 512],
                in0=Y1[:, o_ * 1024 + tt_ * 512: o_ * 1024 + tt_ * 512 + 512],
                in1=psD[:, tt_ * 512:(tt_ + 1) * 512], op=ALU.subtract))(o, tt, q), waits=w_, inc=VE)
            vx2 = p.op("vector", (lambda tt_, q_: lambda e: e.tensor_tensor(
                out=strips[:, q_ * 512: q_ * 512 + 512], in0=strips[:, q_ * 512: q_ * 512 + 512],
                in1=psA[:, tt_ * 512:(tt_ + 1) * 512], op=ALU.mult))(tt, q), inc=VE)
            vd = p.dma("gpsimd", (lambda o_, tt_, q_: lambda e: e.dma_start(
                out=dx2[o_ * 128:(o_ + 1) * 128, tt_ * 512:(tt_ + 1) * 512],
                in_=strips[:, q_ * 512: q_ * 512 + 512]))(o, tt, q), waits=[(VE, vx2)], inc=G)
            vc = p.op("scalar", (lambda o_, tt_, q_: lambda e: e.copy(
                out=actb[:, o_ * 1024 + tt_ * 512: o_ * 1024 + tt_ * 512 + 512],
                in_=strips[:, q_ * 512: q_ * 512 + 512]))(o, tt, q), waits=[(VE, vx2)], inc=SC)
            xsp_done[it] = (vd, vc)
    v_x2b = p.val(SC)
    v_x2g = p.val(G)

    ps_all = (psA, psG, psC, psD)
    for tb in range(8):
        pst = ps_all[tb // 2]
        off = (tb % 2) * 512
        for k in range(NK):
            p.op("tensor", (lambda tb_, k_, pst_, off_: lambda e: e.matmul(
                pst_[:, off_:off_ + 8],
                actb[:, k_ * 1024 + tb_ * 128: k_ * 1024 + (tb_ + 1) * 128],
                wrtb[:, k_ * 8:(k_ + 1) * 8],
                start=(k_ == 0), stop=(k_ == NK - 1)))(tb, k, pst, off),
                waits=[(SC, v_x2b)] if (tb == 0 and k == 0) else (), inc=PE if k == NK - 1 else None)
    v_rt = p.val(PE)
    vlg = 0
    for tb in range(8):
        pst = ps_all[tb // 2]
        off = (tb % 2) * 512
        vlg = p.op("vector", (lambda tb_, pst_, off_: lambda e: e.tensor_copy(
            out=LOGIT[:, tb_ * 8:(tb_ + 1) * 8], in_=pst_[:, off_:off_ + 8]))(tb, pst, off),
            waits=[(PE, v_rt)] if tb == 0 else (), inc=VE)
    MX8 = ET[:, 0:64]
    D8 = SCR[:, 0:8]
    S8 = ET[:, 64:72]
    R8 = ET[:, 72:80]
    for tb in range(8):
        p.op("vector", (lambda tb_: lambda e: e.max(
            out=MX8[:, tb_ * 8:(tb_ + 1) * 8], in_=LOGIT[:, tb_ * 8:(tb_ + 1) * 8]))(tb), inc=VE)
    p.op("vector", lambda e: e.drain(), inc=VE)
    for tb in range(8):
        p.op("vector", (lambda tb_: lambda e: e.tensor_tensor(
            out=D8[:, tb_:tb_ + 1], in0=MX8[:, tb_ * 8 + 1:tb_ * 8 + 2],
            in1=MX8[:, tb_ * 8:tb_ * 8 + 1], op=ALU.subtract))(tb), inc=VE)
    p.op("vector", lambda e: e.drain(), inc=VE)
    v_d8 = p.val(VE)
    ve2 = p.op("scalar", lambda e: e.activation(out=SCR[:, 8:16], in_=D8, func=AF.Exp),
               waits=[(VE, v_d8)], inc=SC)
    p.op("vector", lambda e: e.tensor_scalar(out=S8, in0=SCR[:, 8:16], scalar1=1.0,
                                             scalar2=None, op0=ALU.add),
         waits=[(SC, ve2)], inc=VE)
    p.op("vector", lambda e: e.drain(), inc=VE)
    p.op("vector", lambda e: e.reciprocal(out=R8, in_=S8), inc=VE)
    p.op("vector", lambda e: e.drain(), inc=VE)
    p.op("vector", lambda e: e.tensor_tensor(out=COMB8, in0=S8, in1=R8, op=ALU.mult), inc=VE)
    p.op("vector", lambda e: e.drain(), inc=VE)
    v_c8 = p.val(VE)
    vtp = p.op("tensor", lambda e: e.transpose(psA[0:8, 0:128], COMB8, identf[:]),
               waits=[(VE, v_c8)], inc=PE)
    vev8 = p.op("vector", lambda e: e.tensor_copy(out=comb8T[:], in_=psA[0:8, 0:128]),
                waits=[(PE, vtp)], inc=VE)
    vrow = 0
    for tb in range(8):
        vrow = p.dma("gpsimd", (lambda tb_: lambda e: e.dma_start(
            out=mrow[0:1, tb_ * 128:(tb_ + 1) * 128], in_=comb8T[tb_:tb_ + 1, :]))(tb),
            waits=[(VE, vev8)] if tb == 0 else (), inc=G)
    for tt in range(2):
        p.op("tensor", (lambda tt_: lambda e: e.matmul(
            psC[:, tt_ * 512:(tt_ + 1) * 512], ones1f[:], mrow[0:1, tt_ * 512:(tt_ + 1) * 512],
            start=True, stop=True))(tt), waits=[(G, vrow)] if tt == 0 else (), inc=PE)
    v_cbc_pe = p.val(PE)
    p.op("vector", lambda e: e.tensor_copy(out=CBC[:, 0:512], in_=psC[:, 0:512]),
         waits=[(PE, v_cbc_pe)], inc=VE)
    v_cbc = p.op("vector", lambda e: e.tensor_copy(out=CBC[:, 512:1024], in_=psC[:, 512:1024]), inc=VE)

    silu_done = {}
    h_done = {}
    cast_e = {}
    cast_prev = [wo_cast_prev]
    dn_evac = {}
    dn_pe = {}
    slotfree = {0: mm_slot_done[(1, 14)], 1: mm_slot_done[(1, 15)]}
    first_h_wait = [(PE, v_cbc_pe), (VE, v_cbc)]

    psDfree = [0]
    silu_done = {}
    h_done = {}
    dn_evac = {}
    pe_done = {}
    castv = {}
    gu_prev = {0: None, 1: None}
    slotfree_init = {0: mm_slot_done[(1, 14)], 1: mm_slot_done[(1, 15)]}
    first_h_wait = [(PE, v_cbc_pe), (VE, v_cbc)]

    jobs = []
    for c in range(NCHUNK):
        for il in range(IBC):
            jobs.append(("gu", c * IBC + il, vwsg, vwsu))
        if c == NCHUNK - 1:
            for rib in range(NRIB):
                jobs.append(("gu", 1000 + rib, vwrg, vwru))
        for o in range(NO):
            jobs.append(("dn", c, o))

    def emit_job(k):
        if k >= len(jobs):
            return
        stw = [(VE, castv[k - 1])] if k >= 1 else [wo_cast_prev]
        slw = (PE, pe_done.get(k - 2, slotfree_init[k % 2]))
        csl = (k % 2) * 2
        if jobs[k][0] == "gu":
            _, ib, vg, vu = jobs[k]
            wcol = ib - 1000 if ib >= 1000 else ib
            s1 = p.dma("sync", (lambda v_, w_: lambda e: e.dma_start(
                out=wstage[:, 0:2048], in_=v_[:, :, w_ * 128:(w_ + 1) * 128]))(vg, wcol),
                waits=stw, inc=W)
            s2 = p.dma("sync", (lambda v_, w_: lambda e: e.dma_start(
                out=wstage[:, 2048:4096], in_=v_[:, :, w_ * 128:(w_ + 1) * 128]))(vu, wcol), inc=W)
            p.op("vector", (lambda sl: lambda e: e.tensor_copy(
                out=wcast[:, sl * 2048:(sl + 1) * 2048], in_=wstage[:, 0:2048]))(csl),
                waits=[(W, s1), slw], inc=VE)
            castv[k] = p.op("vector", (lambda sl: lambda e: e.tensor_copy(
                out=wcast[:, (sl + 1) * 2048:(sl + 2) * 2048], in_=wstage[:, 2048:4096]))(csl),
                waits=[(W, s2)], inc=VE)
        else:
            _, c, o = jobs[k]
            s1 = p.dma("sync", (lambda c_, o_: lambda e: e.dma_start(
                out=wstage[:, 0:2048], in_=vwsd[:, c_ * 16:(c_ + 1) * 16, o_ * 128:(o_ + 1) * 128]))(c, o),
                waits=stw, inc=W)
            s2 = None
            if c == NCHUNK - 1:
                s2 = p.dma("sync", (lambda o_: lambda e: e.dma_start(
                    out=wstage[:, 2048:3072], in_=vwrd[:, 0:8, o_ * 128:(o_ + 1) * 128]))(o), inc=W)
            cv = p.op("vector", (lambda sl: lambda e: e.tensor_copy(
                out=wcast[:, sl * 2048:(sl + 1) * 2048], in_=wstage[:, 0:2048]))(csl),
                waits=[(W, s1), slw], inc=VE)
            if s2 is not None:
                cv = p.op("vector", (lambda sl: lambda e: e.tensor_copy(
                    out=wcast[:, (sl + 1) * 2048:(sl + 1) * 2048 + 1024], in_=wstage[:, 2048:3072]))(csl),
                    waits=[(W, s2)], inc=VE)
            castv[k] = cv

    stat_pe2 = {}

    def emit_stats2(o_):
        for tt in range(2):
            it = o_ * 2 + tt
            sl = (it % 2) * 512
            p.op("scalar", (lambda o2, tt2, sl2: lambda e: e.copy(
                out=basebf[:, sl2: sl2 + 512],
                in_=ACC[:, o2 * 1024 + tt2 * 512: o2 * 1024 + tt2 * 512 + 512]))(o_, tt, sl),
                waits=[(VE, dn_evac[(NCHUNK - 1, o_, tt)]), (PE, stat_pe2.get(it - 2, 0)), (G, A_g)], inc=SC)
            s2 = p.op("scalar", (lambda o2, tt2, sl2: lambda e: e.activation(
                out=gbf[:, sl2: sl2 + 512],
                in_=ACC[:, o2 * 1024 + tt2 * 512: o2 * 1024 + tt2 * 512 + 512],
                func=AF.Square))(o_, tt, sl), inc=SC)
            p.op("tensor", (lambda tt2, o2, sl2: lambda e: e.matmul(
                psG[0:1, tt2 * 512:(tt2 + 1) * 512], onesbf[:], basebf[:, sl2: sl2 + 512],
                start=(o2 == 0), stop=(o2 == NO - 1)))(tt, o_, sl), waits=[(SC, s2)], inc=PE)
            p.op("tensor", (lambda tt2, o2, sl2: lambda e: e.matmul(
                psC[0:1, tt2 * 512:(tt2 + 1) * 512], onesbf[:], gbf[:, sl2: sl2 + 512],
                start=(o2 == 0), stop=(o2 == NO - 1)))(tt, o_, sl), inc=PE)
            stat_pe2[it] = p.val(PE)

    emit_job(0)
    for k, job in enumerate(jobs):
        csl = (k % 2) * 2
        if job[0] == "gu":
            _, ib, vg, vu = job
            pair = ib % 2
            psg_ = psA if pair == 0 else psC
            psu_ = psG if pair == 0 else psD
            hcol = (ib - 1000) if ib >= 1000 else (ib % IBC)
            extra_h = ib >= 1000
            prev = gu_prev[pair]
            gw_ = [(VE, castv[k]), (SC, silu_done.get(prev, 0))] + (first_h_wait if ib == 0 else [])
            uw_ = [(VE, h_done.get(prev, 0)), (VE, psDfree[0] if pair == 1 else 0)]
            if ib == 1:
                gw_.append((VE, v_cbc))
                uw_.append((VE, v_cbc))
            for kk in range(NK):
                for tt in range(2):
                    p.op("tensor", (lambda k_, tt_, sl, pg: lambda e: e.matmul(
                        pg[:, tt_ * 512:(tt_ + 1) * 512],
                        wcast[:, sl * 2048 + k_ * 128: sl * 2048 + (k_ + 1) * 128],
                        actb[:, k_ * 1024 + tt_ * 512: k_ * 1024 + (tt_ + 1) * 512],
                        start=(k_ == 0), stop=(k_ == NK - 1)))(kk, tt, csl, psg_),
                        waits=gw_ if (kk == 0 and tt == 0) else (), inc=PE if (kk == NK - 1 and tt == 1) else None)
            for kk in range(NK):
                for tt in range(2):
                    p.op("tensor", (lambda k_, tt_, sl, pu: lambda e: e.matmul(
                        pu[:, tt_ * 512:(tt_ + 1) * 512],
                        wcast[:, (sl + 1) * 2048 + k_ * 128: (sl + 1) * 2048 + (k_ + 1) * 128],
                        actb[:, k_ * 1024 + tt_ * 512: k_ * 1024 + (tt_ + 1) * 512],
                        start=(k_ == 0), stop=(k_ == NK - 1)))(kk, tt, csl, psu_),
                        waits=uw_ if (kk == 0 and tt == 0) else (), inc=PE if (kk == NK - 1 and tt == 1) else None)
            pe_done[k] = p.val(PE)
            emit_job(k + 1)
            ssl = p.op("scalar", (lambda pg: lambda e: e.activation(
                out=strips[:, 0:1024], in_=pg[:, 0:1024], func=AF.Silu))(psg_),
                waits=[(PE, pe_done[k])], inc=SC)
            silu_done[ib] = ssl
            if not extra_h:
                for tt in range(2):
                    vh = p.op("vector", (lambda hc, tt_, pu: lambda e: e.tensor_tensor(
                        out=HH[:, hc * 1024 + tt_ * 512: hc * 1024 + (tt_ + 1) * 512],
                        in0=strips[:, tt_ * 512:(tt_ + 1) * 512],
                        in1=pu[:, tt_ * 512:(tt_ + 1) * 512],
                        op=ALU.mult))(hcol, tt, psu_), waits=[(SC, ssl)] if tt == 0 else (), inc=VE)
            else:
                for tt in range(2):
                    p.op("vector", (lambda tt_, pu: lambda e: e.tensor_tensor(
                        out=strips[:, 1024 + tt_ * 512: 1024 + (tt_ + 1) * 512],
                        in0=strips[:, tt_ * 512:(tt_ + 1) * 512],
                        in1=pu[:, tt_ * 512:(tt_ + 1) * 512], op=ALU.mult))(tt, psu_),
                        waits=[(SC, ssl)] if tt == 0 else (), inc=VE)
                vh = p.op("vector", (lambda hc: lambda e: e.tensor_tensor(
                    out=HH2[:, hc * 1024:(hc + 1) * 1024], in0=strips[:, 1024:2048], in1=CBC,
                    op=ALU.mult))(hcol), inc=VE)
            h_done[ib] = vh
            gu_prev[pair] = ib
        else:
            _, c, o = job
            extra_routed = (c == NCHUNK - 1)
            h_last = h_done[(1000 + NRIB - 1)] if extra_routed else h_done[c * IBC + IBC - 1]
            if c == 0:
                vxr = p.dma("gpsimd", (lambda o_: lambda e: e.dma_start(
                    out=strips[:, 1024:2048],
                    in_=dx2[o_ * 128:(o_ + 1) * 128, 0:1024]))(o),
                    waits=[(G, v_x2g), (VE, dn_evac.get((c, o - 1, 1), 0))], inc=G)
            for tt in range(2):
                mw_ = [(VE, castv[k]), (VE, h_last), (VE, dn_evac.get((c, o - 1, tt), 0))]
                for kk in range(16):
                    p.op("tensor", (lambda kk_, tt_, sl: lambda e: e.matmul(
                        psD[:, tt_ * 512:(tt_ + 1) * 512],
                        wcast[:, sl * 2048 + kk_ * 128: sl * 2048 + (kk_ + 1) * 128],
                        HH[:, kk_ * 1024 + tt_ * 512: kk_ * 1024 + (tt_ + 1) * 512],
                        start=(kk_ == 0), stop=(kk_ == 15 and not extra_routed)))(kk, tt, csl),
                        waits=mw_ if kk == 0 else (), inc=PE if (kk == 15 and not extra_routed) else None)
                if extra_routed:
                    for kk in range(8):
                        p.op("tensor", (lambda kk_, tt_, sl: lambda e: e.matmul(
                            psD[:, tt_ * 512:(tt_ + 1) * 512],
                            wcast[:, (sl + 1) * 2048 + kk_ * 128: (sl + 1) * 2048 + (kk_ + 1) * 128],
                            HH2[:, kk_ * 1024 + tt_ * 512: kk_ * 1024 + (tt_ + 1) * 512],
                            start=False, stop=(kk_ == 7)))(kk, tt, csl), inc=PE if kk == 7 else None)
            pe_done[k] = p.val(PE)
            emit_job(k + 1)
            for tt in range(2):
                if c == 0:
                    vde = p.op("vector", (lambda o_, tt_: lambda e: e.tensor_tensor(
                        out=ACC[:, o_ * 1024 + tt_ * 512: o_ * 1024 + tt_ * 512 + 512],
                        in0=psD[:, tt_ * 512:(tt_ + 1) * 512],
                        in1=strips[:, 1024 + tt_ * 512: 1024 + (tt_ + 1) * 512], op=ALU.add))(o, tt),
                        waits=[(PE, pe_done[k]), (G, vxr)], inc=VE)
                else:
                    vde = p.op("vector", (lambda o_, tt_: lambda e: e.tensor_tensor(
                        out=ACC[:, o_ * 1024 + tt_ * 512: o_ * 1024 + tt_ * 512 + 512],
                        in0=psD[:, tt_ * 512:(tt_ + 1) * 512],
                        in1=ACC[:, o_ * 1024 + tt_ * 512: o_ * 1024 + tt_ * 512 + 512], op=ALU.add))(o, tt),
                        waits=[(PE, pe_done[k])], inc=VE)
                dn_evac[(c, o, tt)] = vde
            if c == NCHUNK - 1:
                if o >= 1:
                    emit_stats2(o - 1)
                if o == NO - 1:
                    emit_stats2(o)
            if o == NO - 1:
                psDfree[0] = p.val(VE)

    v_st2 = stat_pe2[2 * NO - 1]
    p.op("vector", lambda e: e.tensor_scalar_mul(out=mrow, in0=psG[0:1, 0:1024], scalar1=1.0 / D),
         waits=[(PE, v_st2)], inc=VE)
    p.op("vector", lambda e: e.tensor_scalar_mul(out=rstdrow, in0=psC[0:1, 0:1024], scalar1=1.0 / D), inc=VE)
    p.op("vector", lambda e: e.tensor_tensor(out=tmprow, in0=mrow, in1=mrow, op=ALU.mult), inc=VE)
    vvar2 = p.op("vector", lambda e: e.tensor_tensor(out=rstdrow, in0=rstdrow, in1=tmprow, op=ALU.subtract), inc=VE)
    vve2 = p.op("vector", lambda e: e.tensor_scalar(out=tmprow, in0=rstdrow, scalar1=EPS, scalar2=None, op0=ALU.add),
                waits=[(VE, vvar2)], inc=VE)
    vsq2 = p.op("scalar", lambda e: e.activation(out=rstdrow, in_=tmprow, func=AF.Sqrt),
                waits=[(VE, vve2)], inc=SC)
    vrs2 = p.op("vector", lambda e: e.reciprocal(out=tmprow, in_=rstdrow), waits=[(SC, vsq2)], inc=VE)
    for tt in range(2):
        p.op("tensor", (lambda tt_: lambda e: e.matmul(
            psD[:, tt_ * 512:(tt_ + 1) * 512], ones1f[:], mrow[0:1, tt_ * 512:(tt_ + 1) * 512],
            start=True, stop=True))(tt),
            waits=[(VE, vrs2), (PE, v_st2)] if tt == 0 else (), inc=PE)
        p.op("tensor", (lambda tt_: lambda e: e.matmul(
            psA[:, tt_ * 512:(tt_ + 1) * 512], ones1f[:], tmprow[0:1, tt_ * 512:(tt_ + 1) * 512],
            start=True, stop=True))(tt), inc=PE)
    v_bc2 = p.val(PE)
    out_prev = {}
    castv_last = castv[len(jobs) - 1]
    out_q = ["gpsimd", "sync", "scalar"]
    for o in range(NO):
        for tt in range(2):
            it = o * 2 + tt
            q = it % 8
            w_ = [(PE, v_bc2)]
            if it < 8:
                w_.append((VE, castv_last))
            else:
                w_.append((OUT, out_prev[it - 8]))
            vu = p.op("vector", (lambda o_, tt_, q_: lambda e: e.tensor_tensor(
                out=wstage[:, q_ * 512: q_ * 512 + 512],
                in0=ACC[:, o_ * 1024 + tt_ * 512: o_ * 1024 + tt_ * 512 + 512],
                in1=psD[:, tt_ * 512:(tt_ + 1) * 512], op=ALU.subtract))(o, tt, q), waits=w_, inc=VE)
            vo = p.op("vector", (lambda tt_, q_: lambda e: e.tensor_tensor(
                out=wstage[:, q_ * 512: q_ * 512 + 512], in0=wstage[:, q_ * 512: q_ * 512 + 512],
                in1=psA[:, tt_ * 512:(tt_ + 1) * 512], op=ALU.mult))(tt, q), inc=VE)
            vd = p.dma(out_q[it % 3], (lambda o_, tt_, q_: lambda e: e.dma_start(
                out=dout[o_ * 128:(o_ + 1) * 128, tt_ * 512:(tt_ + 1) * 512],
                in_=wstage[:, q_ * 512: q_ * 512 + 512]))(o, tt, q), waits=[(VE, vo)], inc=OUT)
            out_prev[it] = p.val(OUT)
    v_out = p.val(OUT)
    if dbg:
        for (dst0, dst1, src) in ((0, 32, CARRY), (32, 96, EFSB), (96, 104, COMB8),
                                  (232, 248, SCR), (608, 672, LOGIT), (672, 1696, CBC)):
            p.dma("gpsimd", (lambda d0, d1, s_: lambda e: e.dma_start(
                out=ddbg[:, d0:d1], in_=s_))(dst0, dst1, src), waits=[(OUT, v_out)], inc=OUT)
        p.dma("gpsimd", lambda e: e.dma_start(out=ddbg[0:8, 104:232], in_=comb8T[:]), inc=OUT)
        v_out = p.val(OUT)
    p.op("gpsimd", None, waits=[(OUT, v_out)])

    sems = {}
    for ch in (W, G, PE, SC, VE, CC, OUT):
        sems[ch] = es.enter_context(nc.semaphore(ch))

    with nc.Block() as block:
        def emit_for(name):
            hwm = {}

            def go(eng):
                for (e, fn, waits, inc, k) in p.ops:
                    if e != name:
                        continue
                    for (ch, v) in waits:
                        if hwm.get(ch, -1) < v:
                            eng.wait_ge(sems[ch], v)
                            hwm[ch] = v
                    ins = fn(eng) if fn is not None else None
                    if inc is not None:
                        if ins is None:
                            ins = eng.nop()
                        ins.then_inc(sems[inc], k)
            return go

        block.sync(emit_for("sync"))
        block.scalar(emit_for("scalar"))
        block.tensor(emit_for("tensor"))
        block.vector(emit_for("vector"))
        block.gpsimd(emit_for("gpsimd"))

    es.close()
    return nc



def _hostprep(inputs):
    f32 = np.float32
    x = np.asarray(inputs["L0_attn_hidden_states"], f32)
    dt_w = np.asarray(inputs["L0_attn_dt_w"], f32)
    conv_w = np.asarray(inputs["L0_attn_conv_w"], f32)

    shared = {
        "winT": np.ascontiguousarray(np.asarray(inputs["L0_attn_in_proj_w"], f32).T),
        "wgT": np.ascontiguousarray(np.asarray(inputs["L0_attn_gate_proj_w"], f32).T),
        "woT": np.ascontiguousarray(np.asarray(inputs["L0_attn_out_proj_w"], f32).T),
        "wsgT": np.ascontiguousarray(np.asarray(inputs["L0_ffn_sh_gate"], f32).T),
        "wsuT": np.ascontiguousarray(np.asarray(inputs["L0_ffn_sh_up"], f32).T),
        "wsdT": np.ascontiguousarray(np.asarray(inputs["L0_ffn_sh_down"], f32).T),
        "wrgT": np.ascontiguousarray(np.asarray(inputs["L0_ffn_r_gate"], f32).T),
        "wruT": np.ascontiguousarray(np.asarray(inputs["L0_ffn_r_up"], f32).T),
        "wrdT": np.ascontiguousarray(np.asarray(inputs["L0_ffn_r_down"], f32).T),
        "wrtT": np.ascontiguousarray(np.asarray(inputs["L0_ffn_router_weight"], f32).T),
        "ident": np.eye(128, dtype=f32),
    }
    sdt = np.zeros((NCB, 128, 128), f32)
    dtw_r = dt_w.reshape(NCB, 2, 64)
    for h2 in range(2):
        sdt[:, h2 * 64:(h2 + 1) * 64, h2 * 64:(h2 + 1) * 64] = dtw_r[:, h2][:, :, None]
    shared["sdt"] = np.ascontiguousarray(sdt.transpose(1, 0, 2).reshape(128, NCB * 128))
    z = np.zeros((NCB, 4, 128, 128), f32)
    cw_r = conv_w.reshape(NCB, 128, 4)
    z[:, :, np.arange(128), np.arange(128)] = cw_r.transpose(0, 2, 1)[:, :, :]
    shared["diag"] = np.ascontiguousarray(z.transpose(2, 0, 1, 3).reshape(128, NCB * 4 * 128))

    def chancols(v):
        return np.ascontiguousarray(np.asarray(v, f32).reshape(-1).reshape(NCB, 128).T)

    shared["alog"] = chancols(inputs["L0_attn_A_log"])
    shared["bp"] = chancols(inputs["L0_attn_Bp"])
    shared["cp"] = chancols(inputs["L0_attn_Cp"])
    shared["dp"] = chancols(inputs["L0_attn_Dp"])
    shared["convb"] = chancols(inputs["L0_attn_conv_b"])
    shared["dtbc"] = np.ascontiguousarray(
        np.repeat(np.asarray(inputs["L0_attn_dt_b"], f32), 64).reshape(NCB, 128).T)

    in_maps = []
    for c in range(8):
        b, q = c // 4, c % 4
        xs = np.zeros((D, TC), f32)
        if q == 0:
            xs[:, HALO:] = x[b, 0:T, :].T
        else:
            xs[:, :] = x[b, q * T - HALO:(q + 1) * T, :].T
        cm = np.zeros((128, 8), f32)
        for j in range(8):
            if j // 4 == b and (j % 4) < q:
                cm[:, j] = 1.0
        m = dict(shared)
        m["x"] = np.ascontiguousarray(xs)
        m["cmask"] = cm
        in_maps.append(m)
    return in_maps


DBG_MODE = False


def kernel(**inputs):
    in_maps = _hostprep(inputs)
    nc = build_nc(dbg=DBG_MODE)
    res = run_bass_kernel_spmd(nc, in_maps, core_ids=list(range(8)))
    out = np.empty((B, L, D), np.float32)
    for c in range(8):
        b, q = c // 4, c % 4
        out[b, q * T:(q + 1) * T, :] = res.results[c]["o"].T
    if DBG_MODE:
        return out, res.results
    return out


# version 17
# speedup vs baseline: 1.0076x; 1.0055x over previous
import numpy as np
import ml_dtypes
from contextlib import ExitStack

import concourse.bass as bass
import concourse.mybir as mybir
from concourse.bass_utils import run_bass_kernel_spmd

F32 = mybir.dt.float32
BF16 = mybir.dt.bfloat16
AF = mybir.ActivationFunctionType
ALU = mybir.AluOpType
AX = mybir.AxisListType.X
bf16 = ml_dtypes.bfloat16

B, L, D = 2, 4096, 2048
INNER = 4096
T = 1024
HALO = 4
TC = T + HALO
NK = D // 128
NCB = INNER // 128
NO = D // 128
FI = 8192
NIB = FI // 128
NCHUNK = 4
IBC = NIB // NCHUNK
RI = 1024
NRIB = RI // 128
NEG = -1.0e30
EPS = 1e-5


class Prog:

    def __init__(self):
        self.ops = []
        self.cnt = {}

    def val(self, ch):
        return self.cnt.get(ch, 0)

    def op(self, eng, fn=None, waits=(), inc=None, k=1):
        w = tuple((c, v) for (c, v) in waits if v and v > 0)
        self.ops.append((eng, fn, w, inc, k))
        if inc is not None:
            self.cnt[inc] = self.cnt.get(inc, 0) + k
            return self.cnt[inc]
        return None

    def dma(self, eng, fn, waits=(), inc=None):
        return self.op(eng, fn, waits, inc, 16)


_ACT_FALLBACK = {"silu_and_others": 18, "natural_log_exp_and_others": 6, "sqrt_and_others": 3}


def _act_set_id(name):
    try:
        from concourse.hw_specs import get_activation_tables
        keys = list(get_activation_tables("TRN2").keys())
        return keys.index(name)
    except Exception:
        pass
    try:
        import json as _json
        import os as _os
        import neuronxcc
        p = _os.path.join(_os.path.dirname(neuronxcc.__file__), "pwp", "pwp_bin_trainium", "act_info.json")
        sets = _json.load(open(p))["act_func_sets"]
        return [s["name"] for s in sets].index(name)
    except Exception:
        return _ACT_FALLBACK[name]


def build_nc(dbg=False):
    nc = bass.Bass("TRN2", target_bir_lowering=False, debug=False)
    ikind = {"kind": "ExternalOutput"} if dbg else {}

    dx = nc.dram_tensor("x", [D, TC], F32, kind="ExternalInput").ap()
    dwin = nc.dram_tensor("winT", [D, INNER], F32, kind="ExternalInput").ap()
    dwg = nc.dram_tensor("wgT", [D, INNER], F32, kind="ExternalInput").ap()
    dwo = nc.dram_tensor("woT", [INNER, D], F32, kind="ExternalInput").ap()
    dwsg = nc.dram_tensor("wsgT", [D, FI], F32, kind="ExternalInput").ap()
    dwsu = nc.dram_tensor("wsuT", [D, FI], F32, kind="ExternalInput").ap()
    dwsd = nc.dram_tensor("wsdT", [FI, D], F32, kind="ExternalInput").ap()
    dwrg = nc.dram_tensor("wrgT", [D, RI], F32, kind="ExternalInput").ap()
    dwru = nc.dram_tensor("wruT", [D, RI], F32, kind="ExternalInput").ap()
    dwrd = nc.dram_tensor("wrdT", [RI, D], F32, kind="ExternalInput").ap()
    dwrt = nc.dram_tensor("wrtT", [D, 8], F32, kind="ExternalInput").ap()
    dsdt = nc.dram_tensor("sdt", [128, NCB * 128], F32, kind="ExternalInput").ap()
    ddiag = nc.dram_tensor("diag", [128, NCB * 4 * 128], F32, kind="ExternalInput").ap()
    dalog = nc.dram_tensor("alog", [128, NCB], F32, kind="ExternalInput").ap()
    dbp = nc.dram_tensor("bp", [128, NCB], F32, kind="ExternalInput").ap()
    dcp = nc.dram_tensor("cp", [128, NCB], F32, kind="ExternalInput").ap()
    ddp = nc.dram_tensor("dp", [128, NCB], F32, kind="ExternalInput").ap()
    dconvb = nc.dram_tensor("convb", [128, NCB], F32, kind="ExternalInput").ap()
    ddtb = nc.dram_tensor("dtbc", [128, NCB], F32, kind="ExternalInput").ap()
    dcmask = nc.dram_tensor("cmask", [128, 8], F32, kind="ExternalInput").ap()
    dident = nc.dram_tensor("ident", [128, 128], F32, kind="ExternalInput").ap()
    dout = nc.dram_tensor("o", [D, T], F32, kind="ExternalOutput").ap()

    dspB = nc.dram_tensor("spillB", [2, NCB, 128, 512], BF16, **ikind)
    dspG = nc.dram_tensor("spillG", [2, NCB, 128, 512], BF16, **ikind)
    dx2 = nc.dram_tensor("x2sp", [D, T], F32, **ikind)
    defin = nc.dram_tensor("efin", [128, 64], F32)
    defout = nc.dram_tensor("efout", [8, 128, 64], F32)
    ddbg = nc.dram_tensor("dbg", [128, 8192], F32, kind="ExternalOutput").ap() if dbg else None
    ddbg2 = nc.dram_tensor("dbg2", [128, 2048], BF16, kind="ExternalOutput").ap() if dbg else None

    vwin = dwin.rearrange("(k p) c -> p k c", p=128)
    vwg = dwg.rearrange("(k p) c -> p k c", p=128)
    vwo = dwo.rearrange("(k p) c -> p k c", p=128)
    vwsg = dwsg.rearrange("(k p) c -> p k c", p=128)
    vwsu = dwsu.rearrange("(k p) c -> p k c", p=128)
    vwsd = dwsd.rearrange("(k p) c -> p k c", p=128)
    vwrg = dwrg.rearrange("(k p) c -> p k c", p=128)
    vwru = dwru.rearrange("(k p) c -> p k c", p=128)
    vwrd = dwrd.rearrange("(k p) c -> p k c", p=128)
    vwrt = dwrt.rearrange("(k p) c -> p k c", p=128)
    vx = dx.rearrange("(k p) t -> p k t", p=128)

    es = ExitStack()
    sb = lambda name, shape, dt: es.enter_context(nc.sbuf_tensor(name, shape, dt))
    psb = lambda name, shape: es.enter_context(nc.psum_tensor(name, shape, F32))

    actb = sb("actb", [128, NK * TC], BF16)
    bigb = sb("bigb", [128, 24640], BF16)
    bigf = sb("bigf", [128, 16384], F32)
    wstage = sb("wstage", [128, 4096], F32)
    wcast = sb("wcast", [128, 8192], BF16)
    strips = sb("strips", [128, 2048], F32)
    smallf = sb("smallf", [128, 1952], F32)
    identf = sb("identf", [128, 128], F32)
    comb8T = sb("comb8T", [8, 128], F32)
    ones1f = sb("ones1f", [1, 128], F32)
    onesbf = sb("onesbf", [128, 1], BF16)
    wrtb = sb("wrtb", [128, 128], BF16)
    rows = sb("rows", [1, 2048], F32)
    basebf = sb("basebf", [128, 1024], BF16)
    gbf = sb("gbf", [128, 1024], BF16)

    psA = psb("psA", [128, 1024])
    psG = psb("psG", [128, 1024])
    psC = psb("psC", [128, 1024])
    psD = psb("psD", [128, 1024])

    ACOL = smallf[:, 0:32]
    BPC = smallf[:, 32:64]
    CPH = smallf[:, 64:96]
    DPH = smallf[:, 96:128]
    CVB = smallf[:, 128:160]
    DTB = smallf[:, 160:192]
    CMASK = smallf[:, 192:200]
    ONESA = smallf[:, 200:232]
    CARRY = smallf[:, 232:264]
    LOGIT = smallf[:, 264:328]
    COMB8 = smallf[:, 328:336]
    EFSB = smallf[:, 336:400]
    GATH = smallf[:, 400:912]
    CBC = smallf[:, 912:1936]
    SCR = smallf[:, 1936:1952]

    ET = bigf[:, 10240:11264]
    EP1 = bigf[:, 11264:12288]
    SLOC = bigf[:, 12288:13312]
    PP = bigf[:, 13312:14336]
    DRV = bigf[:, 14336:15360]
    Y1 = bigf[:, 0:16384]
    ACC = bigf[:, 0:16384]

    DIAGB = bigb[:, 0:16384]
    SDTB = bigb[:, 16384:20480]
    PROJB = bigb[:, 20480:21508]
    XB = bigb[:, 21536:23584]
    COMBINED = bigb[:, 0:16384]
    BLB8 = bigb[:, 16384:20480]
    BLG8 = bigb[:, 20480:24576]
    STATB = bigb[:, 21536:22560]
    HH = bigb[:, 0:16384]
    HH2 = bigb[:, 16384:24576]

    ZEROS = strips[:, 0:1024]

    SET_SILU = _act_set_id("silu_and_others")
    SET_NLEXP = _act_set_id("natural_log_exp_and_others")
    SET_SQRT = _act_set_id("sqrt_and_others")

    def actload(set_id):
        def fn(s):
            return s.add_instruction(mybir.InstLoadActFuncSet(
                name=nc.get_next_instruction_name(), act_func_set_id=set_id,
                ins=[], outs=[]))
        return fn

    p = Prog()
    W, G, PE, SC, VE, CC, OUT = "wld", "gld", "pe", "sc", "ve", "cc", "out"
    O2, O3 = "out2", "out3"

    for ap_dst, ap_src in ((smallf[:, 0:32], dalog), (BPC, dbp), (smallf[:, 64:96], dcp),
                           (smallf[:, 96:128], ddp), (CVB, dconvb), (DTB, ddtb),
                           (CMASK, dcmask), (identf[:], dident)):
        p.dma("sync", (lambda d, s: lambda e: e.dma_start(out=d, in_=s))(ap_dst, ap_src), inc=W)
    v_small = p.val(W)
    p.dma("sync", lambda e: e.dma_start(out=wstage[:, 0:128], in_=vwrt), inc=W)
    v_wrt = p.val(W)
    p.op("scalar", lambda e: e.copy(out=wrtb[:], in_=wstage[:, 0:128]), waits=[(W, v_wrt)], inc=SC)
    p.op("vector", lambda e: e.memset(ONESA, 1.0), inc=VE)
    p.op("vector", lambda e: e.memset(onesbf[:], 1.0), inc=VE)
    p.op("vector", lambda e: e.memset(ones1f[:], 1.0), inc=VE)
    p.op("vector", lambda e: e.memset(ZEROS, 0.0), inc=VE)
    v_zero = p.val(VE)
    vz = p.dma("gpsimd", lambda e: e.dma_start(out=defin.ap(), in_=ZEROS[:, 0:64]),
               waits=[(VE, v_zero)], inc=G)
    p.op("gpsimd", lambda e: e.collective_compute(
        "AllGather", ALU.bypass, replica_groups=[list(range(8))],
        ins=[defin.ap()], outs=[defout.ap()]), waits=[(G, vz)], inc=CC)
    p.op("scalar", lambda e: e.activation(out=LOGIT[:, 0:32], in_=smallf[:, 0:32], func=AF.Exp),
         waits=[(W, v_small)], inc=SC)
    v_sc0 = p.val(SC)
    p.op("vector", lambda e: e.tensor_scalar_mul(out=ACOL, in0=LOGIT[:, 0:32], scalar1=-1.0),
         waits=[(SC, v_sc0)], inc=VE)
    p.op("vector", lambda e: e.tensor_scalar_mul(out=CPH, in0=smallf[:, 64:96], scalar1=0.5),
         waits=[(W, v_small)], inc=VE)
    p.op("vector", lambda e: e.tensor_scalar_mul(out=DPH, in0=smallf[:, 96:128], scalar1=0.5),
         waits=[(W, v_small)], inc=VE)
    v_const = p.val(VE)

    cast_done = [0] * 64
    ci = 0
    stage_jobs = []
    for k in range(NK):
        stage_jobs.append((vx[:, k, :], TC, actb[:, k * TC:(k + 1) * TC]))
    for j in range(8):
        stage_jobs.append((ddiag[:, j * 2048:(j + 1) * 2048], 2048, DIAGB[:, j * 2048:(j + 1) * 2048]))
    for j in range(2):
        stage_jobs.append((dsdt[:, j * 2048:(j + 1) * 2048], 2048, SDTB[:, j * 2048:(j + 1) * 2048]))
    wrt_cast = p.val(SC)
    for (src, w_, dst) in stage_jobs:
        slot = ci % 2
        wait_cast = cast_done[ci - 2] if ci >= 2 else wrt_cast
        v = p.dma("sync", (lambda s_, w2, sl: lambda e: e.dma_start(out=wstage[:, sl * 2048: sl * 2048 + w2], in_=s_))(src, w_, slot),
                  waits=[(SC, wait_cast)], inc=W)
        cast_done[ci] = p.op("scalar", (lambda d_, w2, sl: lambda e: e.copy(out=d_, in_=wstage[:, sl * 2048: sl * 2048 + w2]))(dst, w_, slot),
                             waits=[(W, v)], inc=SC)
        ci += 1
    v_setup_sc = p.val(SC)
    v_setup_w = p.val(W)

    def w_slice(view, cb_, width=128):
        return view[:, :, cb_ * width:(cb_ + 1) * width]

    vv = p.dma("sync", (lambda: lambda e: e.dma_start(out=wstage[:, 0:2048], in_=w_slice(vwin, 0)))(),
               waits=[(SC, v_setup_sc)], inc=W)
    vv2 = p.dma("sync", (lambda: lambda e: e.dma_start(out=wstage[:, 2048:4096], in_=w_slice(vwg, 0)))(), inc=W)
    c_win = p.op("scalar", lambda e: e.copy(out=wcast[:, 0:2048], in_=wstage[:, 0:2048]), waits=[(W, vv)], inc=SC)
    c_wg = p.op("scalar", lambda e: e.copy(out=wcast[:, 2048:4096], in_=wstage[:, 2048:4096]), waits=[(W, vv2)], inc=SC)

    A = {}

    def rec(cb_, key, v):
        A[(cb_, key)] = v

    def get(cb_, key, default=0):
        return A.get((cb_, key), default)

    def w_slice2(view, cb_):
        return view[:, :, cb_ * 128:(cb_ + 1) * 128]

    rec(0, "cast_w", (SC, c_win))
    rec(0, "cast_g", (SC, c_wg))
    if NCB > 1:
        vs1 = p.dma("sync", lambda e: e.dma_start(out=wstage[:, 0:2048], in_=w_slice2(vwin, 1)),
                    waits=[(SC, c_wg)], inc=W)
        vs2 = p.dma("sync", lambda e: e.dma_start(out=wstage[:, 2048:4096], in_=w_slice2(vwg, 1)), inc=W)
        rec(1, "staged", (vs1, vs2))

    for i in range(NCB + 2):
        j1 = i - 1
        j2 = i - 2
        par_i = i % 2
        if i + 1 <= NCB - 1 and (i + 1, "staged") in A and (i + 1, "cast_w") not in A:
            s1, s2 = A[(i + 1, "staged")]
            npar = (i + 1) % 2
            cwv = p.op("scalar", (lambda sl: lambda e: e.copy(
                out=wcast[:, (2 * sl) * 2048:(2 * sl) * 2048 + 2048], in_=wstage[:, 0:2048]))(npar),
                waits=[(W, s1), (PE, get(i - 1, "pe_gate"))], inc=SC)
            cgv = p.op("scalar", (lambda sl: lambda e: e.copy(
                out=wcast[:, (2 * sl + 1) * 2048:(2 * sl + 1) * 2048 + 2048], in_=wstage[:, 2048:4096]))(npar),
                waits=[(W, s2)], inc=SC)
            rec(i + 1, "cast_w", (SC, cwv))
            rec(i + 1, "cast_g", (SC, cgv))
        if i < NCB:
            winb = wcast[:, (2 * par_i) * 2048:(2 * par_i) * 2048 + 2048]
            wgb = wcast[:, (2 * par_i + 1) * 2048:(2 * par_i + 1) * 2048 + 2048]
            hw = [A[(i, "cast_w")], (SC, get(i - 3, "e")), (SC, get(i - 1, "evacp"))]
            for k in range(NK):
                p.op("tensor", (lambda k_, wb_: lambda e: e.matmul(
                    psD[:, 0:4], wb_[:, k_ * 128:(k_ + 1) * 128],
                    actb[:, k_ * TC: k_ * TC + HALO],
                    start=(k_ == 0), stop=(k_ == NK - 1)))(k, winb),
                    waits=hw if k == 0 else (), inc=PE if k == NK - 1 else None)
            for k in range(NK):
                p.op("tensor", (lambda k_, wb_: lambda e: e.matmul(
                    psA[:, 0:512], wb_[:, k_ * 128:(k_ + 1) * 128],
                    actb[:, k_ * TC + HALO: k_ * TC + HALO + 512],
                    start=(k_ == 0), stop=(k_ == NK - 1)))(k, winb), inc=PE if k == NK - 1 else None)
                p.op("tensor", (lambda k_, wb_: lambda e: e.matmul(
                    psA[:, 512:1024], wb_[:, k_ * 128:(k_ + 1) * 128],
                    actb[:, k_ * TC + HALO + 512: k_ * TC + TC],
                    start=(k_ == 0), stop=(k_ == NK - 1)))(k, winb), inc=PE if k == NK - 1 else None)
            rec(i, "pe_proj", p.val(PE))
            gw = [A[(i, "cast_g")], (SC, get(i - 1, "thg"))]
            for k in range(NK):
                p.op("tensor", (lambda k_, wb_: lambda e: e.matmul(
                    psG[:, 0:512], wb_[:, k_ * 128:(k_ + 1) * 128],
                    actb[:, k_ * TC + HALO: k_ * TC + HALO + 512],
                    start=(k_ == 0), stop=(k_ == NK - 1)))(k, wgb),
                    waits=gw if k == 0 else (), inc=PE if k == NK - 1 else None)
                p.op("tensor", (lambda k_, wb_: lambda e: e.matmul(
                    psG[:, 512:1024], wb_[:, k_ * 128:(k_ + 1) * 128],
                    actb[:, k_ * TC + HALO + 512: k_ * TC + TC],
                    start=(k_ == 0), stop=(k_ == NK - 1)))(k, wgb), inc=PE if k == NK - 1 else None)
            rec(i, "pe_gate", p.val(PE))
        if 0 <= j1 < NCB:
            cw = [(SC, get(j1, "evacp")), (SC, get(j1 - 1, "X"))]
            for tt in range(2):
                for k in range(4):
                    p.op("tensor", (lambda tt_, k_, cb_: lambda e: e.matmul(
                        psC[:, tt_ * 512:(tt_ + 1) * 512],
                        DIAGB[:, (cb_ * 4 + k_) * 128:(cb_ * 4 + k_ + 1) * 128],
                        PROJB[:, 1 + k_ + tt_ * 512: 513 + k_ + tt_ * 512],
                        start=(k_ == 0), stop=(k_ == 3)))(tt, k, j1),
                        waits=cw if (tt == 0 and k == 0) else (), inc=PE if k == 3 else None)
            rec(j1, "pe_conv", p.val(PE))
        if i < NCB:
            ew = [(PE, get(i, "pe_proj")), (PE, get(i - 1, "pe_conv"))]
            p.op("scalar", lambda e: e.copy(out=PROJB[:, 0:HALO], in_=psD[:, 0:4]), waits=ew, inc=SC)
            rec(i, "evacp", p.op("scalar", lambda e: e.copy(out=PROJB[:, HALO:TC], in_=psA[:, 0:1024]), inc=SC))
        if 0 <= j2 < NCB:
            dtw = [(SC, get(j2, "Xb")), (SC, get(j2 - 1, "e")), (SC, get(min(i, NCB - 1), "evacp"))]
            for tt in range(2):
                p.op("tensor", (lambda tt_, cb_: lambda e: e.matmul(
                    psD[:, tt_ * 512:(tt_ + 1) * 512],
                    SDTB[:, cb_ * 128:(cb_ + 1) * 128],
                    XB[:, (cb_ % 2) * 1024 + tt_ * 512: (cb_ % 2) * 1024 + (tt_ + 1) * 512],
                    start=True, stop=True))(tt, j2),
                    waits=dtw if tt == 0 else (), inc=PE if tt == 1 else None)
            rec(j2, "pe_dt", p.val(PE))
        if 0 <= j2 < NCB:
            rec(j2, "e", p.op("scalar", (lambda cb_: lambda e: e.activation(
                out=ET, in_=psD[:, 0:1024], func=AF.Exp, bias=DTB[:, cb_:cb_ + 1], scale=1.0))(j2),
                waits=[(PE, get(j2, "pe_dt")), (VE, get(j2 - 1, "ve_basev"))], inc=SC))
            rec(j2, "ep1", p.op("vector", lambda e: e.tensor_scalar(
                out=EP1, in0=ET, scalar1=1.0, scalar2=None, op0=ALU.add),
                waits=[(SC, get(j2, "e"))], inc=VE))
            dtb_off = 6144 + (j2 % 2) * 1024
            dec_off = 8192 + (j2 % 2) * 1024
            p.op("scalar", (lambda off: lambda e: e.activation(
                out=bigf[:, off: off + 1024], in_=EP1, func=AF.Ln))(dtb_off),
                waits=[(VE, get(j2, "ep1")), (VE, get(j2 - 2, "ve_drive"))], inc=SC)
            rec(j2, "decay", p.op("scalar", (lambda doff, soff, cb_: lambda e: e.activation(
                out=bigf[:, doff: doff + 1024], in_=bigf[:, soff: soff + 1024],
                func=AF.Exp, scale=ACOL[:, cb_:cb_ + 1]))(dec_off, dtb_off, j2),
                waits=[(VE, get(j2 - 2, "ve_pp"))], inc=SC))
        if i < NCB:
            th_sl = i % 4
            thw = [(PE, get(i, "pe_gate")), (VE, get(i - 4, "ve_g"))]
            rec(i, "thg", p.op("scalar", (lambda sl: lambda e: e.activation(
                out=bigf[:, sl * 1024: sl * 1024 + 1024], in_=psG[:, 0:1024], func=AF.Tanh, scale=0.5))(th_sl),
                waits=thw, inc=SC))
        if 0 <= j1 < NCB:
            x_sl = 4096 + (j1 % 2) * 1024
            xw = [(PE, get(j1, "pe_conv")), (VE, get(j1 - 2, "ve_basev"))]
            rec(j1, "X", p.op("scalar", (lambda off, cb_: lambda e: e.activation(
                out=bigf[:, off: off + 1024], in_=psC[:, 0:1024],
                func=AF.Silu, bias=CVB[:, cb_:cb_ + 1], scale=1.0))(x_sl, j1), waits=xw, inc=SC))
            rec(j1, "Xb", p.op("scalar", (lambda off, cb_: lambda e: e.copy(
                out=XB[:, (cb_ % 2) * 1024:(cb_ % 2) * 1024 + 1024], in_=bigf[:, off: off + 1024]))(x_sl, j1),
                waits=[(PE, get(j1 - 2, "pe_dt"))], inc=SC))
        if i + 2 <= NCB - 1:
            ws = [A[(i + 1, "cast_g")] if (i + 1, "cast_g") in A else (SC, 0)]
            s1 = p.dma("sync", (lambda cb_: lambda e: e.dma_start(
                out=wstage[:, 0:2048], in_=w_slice2(vwin, cb_)))(i + 2), waits=ws, inc=W)
            s2 = p.dma("sync", (lambda cb_: lambda e: e.dma_start(
                out=wstage[:, 2048:4096], in_=w_slice2(vwg, cb_)))(i + 2), inc=W)
            rec(i + 2, "staged", (s1, s2))
        if 0 <= j2 < NCB:
            dtb_t = bigf[:, dtb_off: dtb_off + 1024]
            dec_t = bigf[:, dec_off: dec_off + 1024]
            x_t = bigf[:, 4096 + (j2 % 2) * 1024: 4096 + (j2 % 2) * 1024 + 1024]
            th_t = bigf[:, (j2 % 4) * 1024: (j2 % 4) * 1024 + 1024]
            rec(j2, "ve_drive", p.op("vector", (lambda dtb_, x_, cb_: lambda e: e.scalar_tensor_tensor(
                out=DRV, in0=dtb_, scalar=BPC[:, cb_:cb_ + 1], in1=x_, op0=ALU.mult, op1=ALU.mult))(dtb_t, x_t, j2),
                waits=[(SC, get(j2, "decay"))], inc=VE))
            p.op("vector", (lambda dec_: lambda e: e.tensor_tensor_scan(
                out=SLOC, data0=dec_, data1=DRV, initial=0.0, op0=ALU.mult, op1=ALU.add))(dec_t), inc=VE)
            rec(j2, "ve_pp", p.op("vector", (lambda dec_: lambda e: e.tensor_tensor_scan(
                out=PP, data0=dec_, data1=ZEROS, initial=1.0, op0=ALU.mult, op1=ALU.add))(dec_t), inc=VE))
            p.op("vector", (lambda cb_: lambda e: e.tensor_copy(out=EFSB[:, cb_:cb_ + 1], in_=PP[:, 1023:1024]))(j2), inc=VE)
            p.op("vector", (lambda cb_: lambda e: e.tensor_copy(out=EFSB[:, 32 + cb_:32 + cb_ + 1], in_=SLOC[:, 1023:1024]))(j2), inc=VE)
            p.op("vector", (lambda cb_: lambda e: e.tensor_scalar_mul(out=EP1, in0=SLOC, scalar1=CPH[:, cb_:cb_ + 1]))(j2), inc=VE)
            rec(j2, "ve_basev", p.op("vector", (lambda x_, cb_: lambda e: e.scalar_tensor_tensor(
                out=ET, in0=x_, scalar=DPH[:, cb_:cb_ + 1], in1=EP1, op0=ALU.mult, op1=ALU.add))(x_t, j2), inc=VE))
            rec(j2, "ve_base", p.op("vector", (lambda th_: lambda e: e.scalar_tensor_tensor(
                out=basebf[:], in0=th_, scalar=1.0, in1=ET, op0=ALU.add, op1=ALU.mult))(th_t),
                waits=[(G, get(j2 - 1, "spillB"))], inc=VE))
            p.op("vector", (lambda cb_: lambda e: e.tensor_scalar_mul(out=EP1, in0=PP, scalar1=CPH[:, cb_:cb_ + 1]))(j2), inc=VE)
            rec(j2, "ve_g", p.op("vector", (lambda th_: lambda e: e.scalar_tensor_tensor(
                out=gbf[:], in0=th_, scalar=1.0, in1=EP1, op0=ALU.add, op1=ALU.mult))(th_t),
                waits=[(G, get(j2 - 1, "spillG"))], inc=VE))
            for tt in range(2):
                p.dma("gpsimd", (lambda tt_, cb_: lambda e: e.dma_start(
                    out=dspB.ap()[tt_, cb_], in_=basebf[:, tt_ * 512:(tt_ + 1) * 512]))(tt, j2),
                    waits=[(VE, get(j2, "ve_base"))] if tt == 0 else (), inc=G)
            rec(j2, "spillB", p.val(G))
            for tt in range(2):
                p.dma("gpsimd", (lambda tt_, cb_: lambda e: e.dma_start(
                    out=dspG.ap()[tt_, cb_], in_=gbf[:, tt_ * 512:(tt_ + 1) * 512]))(tt, j2),
                    waits=[(VE, get(j2, "ve_g"))] if tt == 0 else (), inc=G)
            rec(j2, "spillG", p.val(G))

    A_pe = p.val(PE)
    A_sc = p.val(SC)
    A_ve = p.val(VE)
    if dbg:
        for (d0, src_) in ((1184, bigf[:, 9216:10240]), (2208, bigf[:, 7168:8192]),
                           (3232, bigf[:, 5120:6144]), (4256, bigf[:, 3072:4096]),
                           (7328, strips[:, 0:512]), (7840, smallf[:, 0:232])):
            w_ = src_.shape[1]
            p.dma("gpsimd", (lambda a, s_, w2: lambda e: e.dma_start(
                out=ddbg[:, a:a + w2], in_=s_))(d0, src_, w_), waits=[(VE, A_ve)], inc=G)
        p.dma("gpsimd", lambda e: e.dma_start(out=ddbg2[:, 0:1028], in_=PROJB), waits=[(VE, A_ve), (SC, A_sc)], inc=G)
    A_g = p.val(G)

    v = p.dma("gpsimd", lambda e: e.dma_start(out=defin.ap(), in_=EFSB), waits=[(VE, A_ve)], inc=G)
    cc1 = p.op("gpsimd", lambda e: e.collective_compute(
        "AllGather", ALU.bypass, replica_groups=[list(range(8))],
        ins=[defin.ap()], outs=[defout.ap()]), waits=[(G, v)], inc=CC)
    vg2 = p.dma("gpsimd", lambda e: e.dma_start(
        out=GATH, in_=defout.ap().rearrange("j p c -> p j c")), waits=[(CC, cc1)], inc=G)

    p.op("vector", lambda e: e.memset(CARRY, 0.0), waits=[(G, vg2)], inc=VE)
    for j in range(8):
        ej = GATH[:, j * 64: j * 64 + 32]
        fj = GATH[:, j * 64 + 32: j * 64 + 64]
        mj = CMASK[:, j:j + 1]
        p.op("vector", (lambda ej_: lambda e: e.tensor_scalar(
            out=ET[:, 0:32], in0=ej_, scalar1=-1.0, scalar2=None, op0=ALU.add))(ej), inc=VE)
        p.op("vector", lambda e: e.drain(), inc=VE)
        p.op("vector", (lambda mj_: lambda e: e.scalar_tensor_tensor(
            out=ET[:, 64:96], in0=ET[:, 0:32], scalar=mj_, in1=ONESA, op0=ALU.mult, op1=ALU.add))(mj), inc=VE)
        p.op("vector", (lambda fj_, mj_: lambda e: e.tensor_scalar_mul(
            out=ET[:, 32:64], in0=fj_, scalar1=mj_))(fj, mj), inc=VE)
        p.op("vector", lambda e: e.drain(), inc=VE)
        p.op("vector", lambda e: e.tensor_tensor(out=CARRY, in0=CARRY, in1=ET[:, 64:96], op=ALU.mult), inc=VE)
        p.op("vector", lambda e: e.drain(), inc=VE)
        p.op("vector", lambda e: e.tensor_tensor(out=CARRY, in0=CARRY, in1=ET[:, 32:64], op=ALU.add), inc=VE)
        p.op("vector", lambda e: e.drain(), inc=VE)
    p.op("vector", lambda e: e.tensor_copy(out=ET[:, 64:96], in_=CARRY), inc=VE)
    p.op("vector", lambda e: e.tensor_copy(out=ET[:, 64:96], in_=CARRY), inc=VE)
    v_carry = p.val(VE)

    mm_slot_done = {}
    evac_done = {}
    cast_done_c = {}
    stats_pe = 0
    wov = dwo.rearrange("(k p) c -> p k c", p=128)
    fix_done = {}
    wo_cast_prev = A[(NCB - 1, "cast_g")]
    for tt in range(2):
        for r4 in range(8):
            par4 = r4 % 2
            blb = bigb[:, 16384 + par4 * 2048: 16384 + par4 * 2048 + 2048]
            blg = bigb[:, 20480 + par4 * 2048: 20480 + par4 * 2048 + 2048]
            wb_ = [(PE, A_pe), (G, A_g)]
            if tt == 1:
                wb_.append((PE, stats_pe))
                wb_.append((VE, evac_done.get((0, 15), 0)))
            prevkey = (tt, r4 - 2) if r4 >= 2 else ((tt - 1, 6 + r4) if tt > 0 else None)
            if prevkey is not None:
                wb_.append((VE, fix_done.get(prevkey, 0)))
            vB = p.dma("gpsimd", (lambda tt_, r_, d_: lambda e: e.dma_start(
                out=d_, in_=dspB.ap()[tt_, r_ * 4:(r_ + 1) * 4].rearrange("b p t -> p b t")))(tt, r4, blb),
                waits=wb_, inc=G)
            vG2 = p.dma("gpsimd", (lambda tt_, r_, d_: lambda e: e.dma_start(
                out=d_, in_=dspG.ap()[tt_, r_ * 4:(r_ + 1) * 4].rearrange("b p t -> p b t")))(tt, r4, blg), inc=G)
            for i in range(4):
                cb = r4 * 4 + i
                p.op("vector", (lambda i_, cb_, bb_, gg_: lambda e: e.scalar_tensor_tensor(
                    out=COMBINED[:, cb_ * 512:(cb_ + 1) * 512],
                    in0=gg_[:, i_ * 512:(i_ + 1) * 512], scalar=CARRY[:, cb_:cb_ + 1],
                    in1=bb_[:, i_ * 512:(i_ + 1) * 512], op0=ALU.mult, op1=ALU.add))(i, cb, blb, blg),
                    waits=[(G, vG2), (VE, v_carry)] if i == 0 else (), inc=VE)
            fix_done[(tt, r4)] = p.val(VE)
        v_fix = p.val(VE)

        for o in range(NO):
            sw = [wo_cast_prev]
            vh0 = p.dma("sync", (lambda o_: lambda e: e.dma_start(
                out=wstage[:, 0:2048], in_=wov[:, 0:16, o_ * 128:(o_ + 1) * 128]))(o), waits=sw, inc=W)
            vh1 = p.dma("sync", (lambda o_: lambda e: e.dma_start(
                out=wstage[:, 2048:4096], in_=wov[:, 16:32, o_ * 128:(o_ + 1) * 128]))(o), inc=W)
            csl = (o % 2) * 2
            slot_prev = (tt, o - 2) if o >= 2 else (tt - 1, 14 + o)
            cw0 = p.op("vector", (lambda sl: lambda e: e.tensor_copy(
                out=wcast[:, sl * 2048:(sl + 1) * 2048], in_=wstage[:, 0:2048]))(csl),
                waits=[(W, vh0), (PE, mm_slot_done.get(slot_prev, A_pe))], inc=VE)
            cw1 = p.op("vector", (lambda sl: lambda e: e.tensor_copy(
                out=wcast[:, (sl + 1) * 2048:(sl + 2) * 2048], in_=wstage[:, 2048:4096]))(csl),
                waits=[(W, vh1)], inc=VE)
            wo_cast_prev = (VE, cw1)
            cast_done_c[o] = cw1
            vxs = p.dma("gpsimd", (lambda o_, tt_: lambda e: e.dma_start(
                out=strips[:, (o_ % 2) * 512:(o_ % 2) * 512 + 512],
                in_=dx[o_ * 128:(o_ + 1) * 128, HALO + tt_ * 512: HALO + (tt_ + 1) * 512]))(o, tt),
                waits=[(VE, evac_done.get((tt, o - 2), 0) if (tt, o - 2) in evac_done else (evac_done.get((tt - 1, 14 + o), 0)))], inc=G)
            bank = (o % 2) * 512
            mw = [(VE, cw1), (VE, evac_done.get((tt, o - 2), 0))]
            for kk in range(32):
                sl2 = csl * 2048 + (kk // 16) * 2048 + (kk % 16) * 128
                kw = list(mw) if kk == 0 else []
                if kk % 4 == 0:
                    kw.append((VE, fix_done[(tt, kk // 4)]))
                p.op("tensor", (lambda kk_, sl_, bank_, tt_: lambda e: e.matmul(
                    psA[:, bank_:bank_ + 512], wcast[:, sl_: sl_ + 128],
                    COMBINED[:, kk_ * 512:(kk_ + 1) * 512],
                    start=(kk_ == 0), stop=(kk_ == 31)))(kk, sl2, bank, tt),
                    waits=kw, inc=PE if kk == 31 else None)
            mm_slot_done[(tt, o)] = p.val(PE)
            vev = p.op("vector", (lambda o_, tt_, bank_: lambda e: e.tensor_tensor(
                out=Y1[:, o_ * 1024 + tt_ * 512: o_ * 1024 + tt_ * 512 + 512],
                in0=psA[:, bank_:bank_ + 512], in1=strips[:, (o_ % 2) * 512:(o_ % 2) * 512 + 512],
                op=ALU.add))(o, tt, bank),
                waits=[(PE, mm_slot_done[(tt, o)]), (G, vxs)], inc=VE)
            evac_done[(tt, o)] = vev
            sb1 = p.op("scalar", (lambda o_, tt_: lambda e: e.copy(
                out=STATB[:, 0:512], in_=Y1[:, o_ * 1024 + tt_ * 512: o_ * 1024 + tt_ * 512 + 512]))(o, tt),
                waits=[(VE, vev), (PE, stats_pe)], inc=SC)
            sb2 = p.op("scalar", (lambda o_, tt_: lambda e: e.activation(
                out=STATB[:, 512:1024], in_=Y1[:, o_ * 1024 + tt_ * 512: o_ * 1024 + tt_ * 512 + 512],
                func=AF.Square))(o, tt), inc=SC)
            p.op("tensor", (lambda tt_, o_: lambda e: e.matmul(
                psG[0:1, tt_ * 512:(tt_ + 1) * 512], onesbf[:], STATB[:, 0:512],
                start=(o_ == 0), stop=(o_ == NO - 1)))(tt, o), waits=[(SC, sb2)], inc=PE)
            p.op("tensor", (lambda tt_, o_: lambda e: e.matmul(
                psC[0:1, tt_ * 512:(tt_ + 1) * 512], onesbf[:], STATB[:, 512:1024],
                start=(o_ == 0), stop=(o_ == NO - 1)))(tt, o), inc=PE)
            stats_pe = p.val(PE)

    mrow = rows[0:1, 0:1024]
    rstdrow = rows[0:1, 1024:2048]
    tmprow = CBC[0:1, 0:1024]
    v1 = p.op("vector", lambda e: e.tensor_scalar_mul(out=mrow, in0=psG[0:1, 0:1024], scalar1=1.0 / D),
              waits=[(PE, stats_pe)], inc=VE)
    p.op("vector", lambda e: e.tensor_scalar_mul(out=rstdrow, in0=psC[0:1, 0:1024], scalar1=1.0 / D), inc=VE)
    p.op("vector", lambda e: e.tensor_tensor(out=tmprow, in0=mrow, in1=mrow, op=ALU.mult), inc=VE)
    vvar = p.op("vector", lambda e: e.tensor_tensor(out=rstdrow, in0=rstdrow, in1=tmprow, op=ALU.subtract), inc=VE)
    vve = p.op("vector", lambda e: e.tensor_scalar(out=tmprow, in0=rstdrow, scalar1=EPS, scalar2=None, op0=ALU.add),
               waits=[(VE, vvar)], inc=VE)
    vsq = p.op("scalar", lambda e: e.activation(out=rstdrow, in_=tmprow, func=AF.Sqrt),
               waits=[(VE, vve)], inc=SC)
    vrs = p.op("vector", lambda e: e.reciprocal(out=tmprow, in_=rstdrow), waits=[(SC, vsq)], inc=VE)
    bcw = [(VE, vrs), (PE, stats_pe)]
    for tt in range(2):
        p.op("tensor", (lambda tt_: lambda e: e.matmul(
            psD[:, tt_ * 512:(tt_ + 1) * 512], ones1f[:], mrow[0:1, tt_ * 512:(tt_ + 1) * 512],
            start=True, stop=True))(tt), waits=bcw if tt == 0 else (), inc=PE)
        p.op("tensor", (lambda tt_: lambda e: e.matmul(
            psA[:, tt_ * 512:(tt_ + 1) * 512], ones1f[:], tmprow[0:1, tt_ * 512:(tt_ + 1) * 512],
            start=True, stop=True))(tt), inc=PE)
    v_bc = p.val(PE)

    xsp_done = {}
    for o in range(NO):
        for tt in range(2):
            it = o * 2 + tt
            q = it % 4
            w_ = [(PE, v_bc)]
            if it >= 4:
                prev = xsp_done[it - 4]
                w_ += [(G, prev[0]), (SC, prev[1])]
            vu = p.op("vector", (lambda o_, tt_, q_: lambda e: e.tensor_tensor(
                out=strips[:, q_ * 512: q_ * 512 + 512],
                in0=Y1[:, o_ * 1024 + tt_ * 512: o_ * 1024 + tt_ * 512 + 512],
                in1=psD[:, tt_ * 512:(tt_ + 1) * 512], op=ALU.subtract))(o, tt, q), waits=w_, inc=VE)
            vx2 = p.op("vector", (lambda tt_, q_: lambda e: e.tensor_tensor(
                out=strips[:, q_ * 512: q_ * 512 + 512], in0=strips[:, q_ * 512: q_ * 512 + 512],
                in1=psA[:, tt_ * 512:(tt_ + 1) * 512], op=ALU.mult))(tt, q), inc=VE)
            vd = p.dma("gpsimd", (lambda o_, tt_, q_: lambda e: e.dma_start(
                out=dx2[o_ * 128:(o_ + 1) * 128, tt_ * 512:(tt_ + 1) * 512],
                in_=strips[:, q_ * 512: q_ * 512 + 512]))(o, tt, q), waits=[(VE, vx2)], inc=G)
            vc = p.op("scalar", (lambda o_, tt_, q_: lambda e: e.copy(
                out=actb[:, o_ * 1024 + tt_ * 512: o_ * 1024 + tt_ * 512 + 512],
                in_=strips[:, q_ * 512: q_ * 512 + 512]))(o, tt, q), waits=[(VE, vx2)], inc=SC)
            xsp_done[it] = (vd, vc)
    v_x2b = p.val(SC)
    v_x2g = p.val(G)

    ps_all = (psA, psG, psC, psD)
    for tb in range(8):
        pst = ps_all[tb // 2]
        off = (tb % 2) * 512
        for k in range(NK):
            p.op("tensor", (lambda tb_, k_, pst_, off_: lambda e: e.matmul(
                pst_[:, off_:off_ + 8],
                actb[:, k_ * 1024 + tb_ * 128: k_ * 1024 + (tb_ + 1) * 128],
                wrtb[:, k_ * 8:(k_ + 1) * 8],
                start=(k_ == 0), stop=(k_ == NK - 1)))(tb, k, pst, off),
                waits=[(SC, v_x2b)] if (tb == 0 and k == 0) else (), inc=PE if k == NK - 1 else None)
    v_rt = p.val(PE)
    vlg = 0
    for tb in range(8):
        pst = ps_all[tb // 2]
        off = (tb % 2) * 512
        vlg = p.op("vector", (lambda tb_, pst_, off_: lambda e: e.tensor_copy(
            out=LOGIT[:, tb_ * 8:(tb_ + 1) * 8], in_=pst_[:, off_:off_ + 8]))(tb, pst, off),
            waits=[(PE, v_rt)] if tb == 0 else (), inc=VE)
    MX8 = ET[:, 0:64]
    D8 = SCR[:, 0:8]
    S8 = ET[:, 64:72]
    R8 = ET[:, 72:80]
    for tb in range(8):
        p.op("vector", (lambda tb_: lambda e: e.max(
            out=MX8[:, tb_ * 8:(tb_ + 1) * 8], in_=LOGIT[:, tb_ * 8:(tb_ + 1) * 8]))(tb), inc=VE)
    p.op("vector", lambda e: e.drain(), inc=VE)
    for tb in range(8):
        p.op("vector", (lambda tb_: lambda e: e.tensor_tensor(
            out=D8[:, tb_:tb_ + 1], in0=MX8[:, tb_ * 8 + 1:tb_ * 8 + 2],
            in1=MX8[:, tb_ * 8:tb_ * 8 + 1], op=ALU.subtract))(tb), inc=VE)
    p.op("vector", lambda e: e.drain(), inc=VE)
    v_d8 = p.val(VE)
    ve2 = p.op("scalar", lambda e: e.activation(out=SCR[:, 8:16], in_=D8, func=AF.Exp),
               waits=[(VE, v_d8)], inc=SC)
    p.op("vector", lambda e: e.tensor_scalar(out=S8, in0=SCR[:, 8:16], scalar1=1.0,
                                             scalar2=None, op0=ALU.add),
         waits=[(SC, ve2)], inc=VE)
    p.op("vector", lambda e: e.drain(), inc=VE)
    p.op("vector", lambda e: e.reciprocal(out=R8, in_=S8), inc=VE)
    p.op("vector", lambda e: e.drain(), inc=VE)
    p.op("vector", lambda e: e.tensor_tensor(out=COMB8, in0=S8, in1=R8, op=ALU.mult), inc=VE)
    p.op("vector", lambda e: e.drain(), inc=VE)
    v_c8 = p.val(VE)
    vtp = p.op("tensor", lambda e: e.transpose(psA[0:8, 0:128], COMB8, identf[:]),
               waits=[(VE, v_c8)], inc=PE)
    vev8 = p.op("vector", lambda e: e.tensor_copy(out=comb8T[:], in_=psA[0:8, 0:128]),
                waits=[(PE, vtp)], inc=VE)
    vrow = 0
    for tb in range(8):
        vrow = p.dma("gpsimd", (lambda tb_: lambda e: e.dma_start(
            out=mrow[0:1, tb_ * 128:(tb_ + 1) * 128], in_=comb8T[tb_:tb_ + 1, :]))(tb),
            waits=[(VE, vev8)] if tb == 0 else (), inc=G)
    for tt in range(2):
        p.op("tensor", (lambda tt_: lambda e: e.matmul(
            psC[:, tt_ * 512:(tt_ + 1) * 512], ones1f[:], mrow[0:1, tt_ * 512:(tt_ + 1) * 512],
            start=True, stop=True))(tt), waits=[(G, vrow)] if tt == 0 else (), inc=PE)
    v_cbc_pe = p.val(PE)
    p.op("vector", lambda e: e.tensor_copy(out=CBC[:, 0:512], in_=psC[:, 0:512]),
         waits=[(PE, v_cbc_pe)], inc=VE)
    v_cbc = p.op("vector", lambda e: e.tensor_copy(out=CBC[:, 512:1024], in_=psC[:, 512:1024]), inc=VE)

    silu_done = {}
    h_done = {}
    cast_e = {}
    cast_prev = [wo_cast_prev]
    dn_evac = {}
    dn_pe = {}
    slotfree = {0: mm_slot_done[(1, 14)], 1: mm_slot_done[(1, 15)]}
    first_h_wait = [(PE, v_cbc_pe), (VE, v_cbc)]

    psDfree = [0]
    silu_done = {}
    h_done = {}
    dn_evac = {}
    pe_done = {}
    castv = {}
    gu_prev = {0: None, 1: None}
    slotfree_init = {0: mm_slot_done[(1, 14)], 1: mm_slot_done[(1, 15)]}
    first_h_wait = [(PE, v_cbc_pe), (VE, v_cbc)]

    jobs = []
    for c in range(NCHUNK):
        for il in range(IBC):
            jobs.append(("gu", c * IBC + il, vwsg, vwsu))
        if c == NCHUNK - 1:
            for rib in range(NRIB):
                jobs.append(("gu", 1000 + rib, vwrg, vwru))
        for o in range(NO):
            jobs.append(("dn", c, o))

    def emit_job(k):
        if k >= len(jobs):
            return
        stw = [(VE, castv[k - 1])] if k >= 1 else [wo_cast_prev]
        slw = (PE, pe_done.get(k - 2, slotfree_init[k % 2]))
        csl = (k % 2) * 2
        if jobs[k][0] == "gu":
            _, ib, vg, vu = jobs[k]
            wcol = ib - 1000 if ib >= 1000 else ib
            s1 = p.dma("sync", (lambda v_, w_: lambda e: e.dma_start(
                out=wstage[:, 0:2048], in_=v_[:, :, w_ * 128:(w_ + 1) * 128]))(vg, wcol),
                waits=stw, inc=W)
            s2 = p.dma("sync", (lambda v_, w_: lambda e: e.dma_start(
                out=wstage[:, 2048:4096], in_=v_[:, :, w_ * 128:(w_ + 1) * 128]))(vu, wcol), inc=W)
            p.op("vector", (lambda sl: lambda e: e.tensor_copy(
                out=wcast[:, sl * 2048:(sl + 1) * 2048], in_=wstage[:, 0:2048]))(csl),
                waits=[(W, s1), slw], inc=VE)
            castv[k] = p.op("vector", (lambda sl: lambda e: e.tensor_copy(
                out=wcast[:, (sl + 1) * 2048:(sl + 2) * 2048], in_=wstage[:, 2048:4096]))(csl),
                waits=[(W, s2)], inc=VE)
        else:
            _, c, o = jobs[k]
            s1 = p.dma("sync", (lambda c_, o_: lambda e: e.dma_start(
                out=wstage[:, 0:2048], in_=vwsd[:, c_ * 16:(c_ + 1) * 16, o_ * 128:(o_ + 1) * 128]))(c, o),
                waits=stw, inc=W)
            s2 = None
            if c == NCHUNK - 1:
                s2 = p.dma("sync", (lambda o_: lambda e: e.dma_start(
                    out=wstage[:, 2048:3072], in_=vwrd[:, 0:8, o_ * 128:(o_ + 1) * 128]))(o), inc=W)
            cv = p.op("vector", (lambda sl: lambda e: e.tensor_copy(
                out=wcast[:, sl * 2048:(sl + 1) * 2048], in_=wstage[:, 0:2048]))(csl),
                waits=[(W, s1), slw], inc=VE)
            if s2 is not None:
                cv = p.op("vector", (lambda sl: lambda e: e.tensor_copy(
                    out=wcast[:, (sl + 1) * 2048:(sl + 1) * 2048 + 1024], in_=wstage[:, 2048:3072]))(csl),
                    waits=[(W, s2)], inc=VE)
            castv[k] = cv

    stat_pe2 = {}

    def emit_stats2(o_):
        for tt in range(2):
            it = o_ * 2 + tt
            sl = (it % 2) * 512
            p.op("scalar", (lambda o2, tt2, sl2: lambda e: e.copy(
                out=basebf[:, sl2: sl2 + 512],
                in_=ACC[:, o2 * 1024 + tt2 * 512: o2 * 1024 + tt2 * 512 + 512]))(o_, tt, sl),
                waits=[(VE, dn_evac[(NCHUNK - 1, o_, tt)]), (PE, stat_pe2.get(it - 2, 0)), (G, A_g)], inc=SC)
            s2 = p.op("scalar", (lambda o2, tt2, sl2: lambda e: e.activation(
                out=gbf[:, sl2: sl2 + 512],
                in_=ACC[:, o2 * 1024 + tt2 * 512: o2 * 1024 + tt2 * 512 + 512],
                func=AF.Square))(o_, tt, sl), inc=SC)
            p.op("tensor", (lambda tt2, o2, sl2: lambda e: e.matmul(
                psG[0:1, tt2 * 512:(tt2 + 1) * 512], onesbf[:], basebf[:, sl2: sl2 + 512],
                start=(o2 == 0), stop=(o2 == NO - 1)))(tt, o_, sl), waits=[(SC, s2)], inc=PE)
            p.op("tensor", (lambda tt2, o2, sl2: lambda e: e.matmul(
                psC[0:1, tt2 * 512:(tt2 + 1) * 512], onesbf[:], gbf[:, sl2: sl2 + 512],
                start=(o2 == 0), stop=(o2 == NO - 1)))(tt, o_, sl), inc=PE)
            stat_pe2[it] = p.val(PE)

    emit_job(0)
    for k, job in enumerate(jobs):
        csl = (k % 2) * 2
        if job[0] == "gu":
            _, ib, vg, vu = job
            pair = ib % 2
            psg_ = psA if pair == 0 else psC
            psu_ = psG if pair == 0 else psD
            hcol = (ib - 1000) if ib >= 1000 else (ib % IBC)
            extra_h = ib >= 1000
            prev = gu_prev[pair]
            gw_ = [(VE, castv[k]), (SC, silu_done.get(prev, 0))] + (first_h_wait if ib == 0 else [])
            uw_ = [(VE, h_done.get(prev, 0)), (VE, psDfree[0] if pair == 1 else 0)]
            if ib == 1:
                gw_.append((VE, v_cbc))
                uw_.append((VE, v_cbc))
            for kk in range(NK):
                for tt in range(2):
                    p.op("tensor", (lambda k_, tt_, sl, pg: lambda e: e.matmul(
                        pg[:, tt_ * 512:(tt_ + 1) * 512],
                        wcast[:, sl * 2048 + k_ * 128: sl * 2048 + (k_ + 1) * 128],
                        actb[:, k_ * 1024 + tt_ * 512: k_ * 1024 + (tt_ + 1) * 512],
                        start=(k_ == 0), stop=(k_ == NK - 1)))(kk, tt, csl, psg_),
                        waits=gw_ if (kk == 0 and tt == 0) else (), inc=PE if (kk == NK - 1 and tt == 1) else None)
            for kk in range(NK):
                for tt in range(2):
                    p.op("tensor", (lambda k_, tt_, sl, pu: lambda e: e.matmul(
                        pu[:, tt_ * 512:(tt_ + 1) * 512],
                        wcast[:, (sl + 1) * 2048 + k_ * 128: (sl + 1) * 2048 + (k_ + 1) * 128],
                        actb[:, k_ * 1024 + tt_ * 512: k_ * 1024 + (tt_ + 1) * 512],
                        start=(k_ == 0), stop=(k_ == NK - 1)))(kk, tt, csl, psu_),
                        waits=uw_ if (kk == 0 and tt == 0) else (), inc=PE if (kk == NK - 1 and tt == 1) else None)
            pe_done[k] = p.val(PE)
            emit_job(k + 1)
            ssl = p.op("scalar", (lambda pg: lambda e: e.activation(
                out=strips[:, 0:1024], in_=pg[:, 0:1024], func=AF.Silu))(psg_),
                waits=[(PE, pe_done[k])], inc=SC)
            silu_done[ib] = ssl
            if not extra_h:
                for tt in range(2):
                    vh = p.op("vector", (lambda hc, tt_, pu: lambda e: e.tensor_tensor(
                        out=HH[:, hc * 1024 + tt_ * 512: hc * 1024 + (tt_ + 1) * 512],
                        in0=strips[:, tt_ * 512:(tt_ + 1) * 512],
                        in1=pu[:, tt_ * 512:(tt_ + 1) * 512],
                        op=ALU.mult))(hcol, tt, psu_), waits=[(SC, ssl)] if tt == 0 else (), inc=VE)
            else:
                for tt in range(2):
                    p.op("vector", (lambda tt_, pu: lambda e: e.tensor_tensor(
                        out=strips[:, 1024 + tt_ * 512: 1024 + (tt_ + 1) * 512],
                        in0=strips[:, tt_ * 512:(tt_ + 1) * 512],
                        in1=pu[:, tt_ * 512:(tt_ + 1) * 512], op=ALU.mult))(tt, psu_),
                        waits=[(SC, ssl)] if tt == 0 else (), inc=VE)
                vh = p.op("vector", (lambda hc: lambda e: e.tensor_tensor(
                    out=HH2[:, hc * 1024:(hc + 1) * 1024], in0=strips[:, 1024:2048], in1=CBC,
                    op=ALU.mult))(hcol), inc=VE)
            h_done[ib] = vh
            gu_prev[pair] = ib
        else:
            _, c, o = job
            extra_routed = (c == NCHUNK - 1)
            h_last = h_done[(1000 + NRIB - 1)] if extra_routed else h_done[c * IBC + IBC - 1]
            if c == 0:
                vxr = p.dma("gpsimd", (lambda o_: lambda e: e.dma_start(
                    out=strips[:, 1024:2048],
                    in_=dx2[o_ * 128:(o_ + 1) * 128, 0:1024]))(o),
                    waits=[(G, v_x2g), (VE, dn_evac.get((c, o - 1, 1), 0))], inc=G)
            for tt in range(2):
                mw_ = [(VE, castv[k]), (VE, h_last), (VE, dn_evac.get((c, o - 1, tt), 0))]
                for kk in range(16):
                    p.op("tensor", (lambda kk_, tt_, sl: lambda e: e.matmul(
                        psD[:, tt_ * 512:(tt_ + 1) * 512],
                        wcast[:, sl * 2048 + kk_ * 128: sl * 2048 + (kk_ + 1) * 128],
                        HH[:, kk_ * 1024 + tt_ * 512: kk_ * 1024 + (tt_ + 1) * 512],
                        start=(kk_ == 0), stop=(kk_ == 15 and not extra_routed)))(kk, tt, csl),
                        waits=mw_ if kk == 0 else (), inc=PE if (kk == 15 and not extra_routed) else None)
                if extra_routed:
                    for kk in range(8):
                        p.op("tensor", (lambda kk_, tt_, sl: lambda e: e.matmul(
                            psD[:, tt_ * 512:(tt_ + 1) * 512],
                            wcast[:, (sl + 1) * 2048 + kk_ * 128: (sl + 1) * 2048 + (kk_ + 1) * 128],
                            HH2[:, kk_ * 1024 + tt_ * 512: kk_ * 1024 + (tt_ + 1) * 512],
                            start=False, stop=(kk_ == 7)))(kk, tt, csl), inc=PE if kk == 7 else None)
            pe_done[k] = p.val(PE)
            emit_job(k + 1)
            for tt in range(2):
                if c == 0:
                    vde = p.op("vector", (lambda o_, tt_: lambda e: e.tensor_tensor(
                        out=ACC[:, o_ * 1024 + tt_ * 512: o_ * 1024 + tt_ * 512 + 512],
                        in0=psD[:, tt_ * 512:(tt_ + 1) * 512],
                        in1=strips[:, 1024 + tt_ * 512: 1024 + (tt_ + 1) * 512], op=ALU.add))(o, tt),
                        waits=[(PE, pe_done[k]), (G, vxr)], inc=VE)
                else:
                    vde = p.op("vector", (lambda o_, tt_: lambda e: e.tensor_tensor(
                        out=ACC[:, o_ * 1024 + tt_ * 512: o_ * 1024 + tt_ * 512 + 512],
                        in0=psD[:, tt_ * 512:(tt_ + 1) * 512],
                        in1=ACC[:, o_ * 1024 + tt_ * 512: o_ * 1024 + tt_ * 512 + 512], op=ALU.add))(o, tt),
                        waits=[(PE, pe_done[k])], inc=VE)
                dn_evac[(c, o, tt)] = vde
            if c == NCHUNK - 1:
                if o >= 1:
                    emit_stats2(o - 1)
                if o == NO - 1:
                    emit_stats2(o)
            if o == NO - 1:
                psDfree[0] = p.val(VE)

    v_st2 = stat_pe2[2 * NO - 1]
    p.op("vector", lambda e: e.tensor_scalar_mul(out=mrow, in0=psG[0:1, 0:1024], scalar1=1.0 / D),
         waits=[(PE, v_st2)], inc=VE)
    p.op("vector", lambda e: e.tensor_scalar_mul(out=rstdrow, in0=psC[0:1, 0:1024], scalar1=1.0 / D), inc=VE)
    p.op("vector", lambda e: e.tensor_tensor(out=tmprow, in0=mrow, in1=mrow, op=ALU.mult), inc=VE)
    vvar2 = p.op("vector", lambda e: e.tensor_tensor(out=rstdrow, in0=rstdrow, in1=tmprow, op=ALU.subtract), inc=VE)
    vve2 = p.op("vector", lambda e: e.tensor_scalar(out=tmprow, in0=rstdrow, scalar1=EPS, scalar2=None, op0=ALU.add),
                waits=[(VE, vvar2)], inc=VE)
    vsq2 = p.op("scalar", lambda e: e.activation(out=rstdrow, in_=tmprow, func=AF.Sqrt),
                waits=[(VE, vve2)], inc=SC)
    vrs2 = p.op("vector", lambda e: e.reciprocal(out=tmprow, in_=rstdrow), waits=[(SC, vsq2)], inc=VE)
    for tt in range(2):
        p.op("tensor", (lambda tt_: lambda e: e.matmul(
            psD[:, tt_ * 512:(tt_ + 1) * 512], ones1f[:], mrow[0:1, tt_ * 512:(tt_ + 1) * 512],
            start=True, stop=True))(tt),
            waits=[(VE, vrs2), (PE, v_st2)] if tt == 0 else (), inc=PE)
        p.op("tensor", (lambda tt_: lambda e: e.matmul(
            psA[:, tt_ * 512:(tt_ + 1) * 512], ones1f[:], tmprow[0:1, tt_ * 512:(tt_ + 1) * 512],
            start=True, stop=True))(tt), inc=PE)
    v_bc2 = p.val(PE)
    out_prev = {}
    castv_last = castv[len(jobs) - 1]
    out_q = ["gpsimd", "sync", "scalar"]
    out_ch = [OUT, O2, O3]
    for o in range(NO):
        for tt in range(2):
            it = o * 2 + tt
            q = it % 8
            w_ = [(PE, v_bc2)]
            if it < 8:
                w_.append((VE, castv_last))
            else:
                w_.append(out_prev[it - 8])
            vu = p.op("vector", (lambda o_, tt_, q_: lambda e: e.tensor_tensor(
                out=wstage[:, q_ * 512: q_ * 512 + 512],
                in0=ACC[:, o_ * 1024 + tt_ * 512: o_ * 1024 + tt_ * 512 + 512],
                in1=psD[:, tt_ * 512:(tt_ + 1) * 512], op=ALU.subtract))(o, tt, q), waits=w_, inc=VE)
            vo = p.op("vector", (lambda tt_, q_: lambda e: e.tensor_tensor(
                out=wstage[:, q_ * 512: q_ * 512 + 512], in0=wstage[:, q_ * 512: q_ * 512 + 512],
                in1=psA[:, tt_ * 512:(tt_ + 1) * 512], op=ALU.mult))(tt, q), inc=VE)
            ch_ = out_ch[it % 3]
            vd = p.dma(out_q[it % 3], (lambda o_, tt_, q_: lambda e: e.dma_start(
                out=dout[o_ * 128:(o_ + 1) * 128, tt_ * 512:(tt_ + 1) * 512],
                in_=wstage[:, q_ * 512: q_ * 512 + 512]))(o, tt, q), waits=[(VE, vo)], inc=ch_)
            out_prev[it] = (ch_, p.val(ch_))
    fin_w = [(c, p.val(c)) for c in out_ch]
    if dbg:
        for (dst0, dst1, src) in ((0, 32, CARRY), (32, 96, EFSB), (96, 104, COMB8),
                                  (232, 248, SCR), (608, 672, LOGIT), (672, 1696, CBC)):
            p.dma("gpsimd", (lambda d0, d1, s_: lambda e: e.dma_start(
                out=ddbg[:, d0:d1], in_=s_))(dst0, dst1, src), waits=list(fin_w), inc=OUT)
        p.dma("gpsimd", lambda e: e.dma_start(out=ddbg[0:8, 104:232], in_=comb8T[:]), inc=OUT)
        fin_w = [(OUT, p.val(OUT))]
    p.op("gpsimd", None, waits=fin_w)

    sems = {}
    for ch in (W, G, PE, SC, VE, CC, OUT, O2, O3):
        sems[ch] = es.enter_context(nc.semaphore(ch))

    with nc.Block() as block:
        def emit_for(name):
            hwm = {}

            def go(eng):
                for (e, fn, waits, inc, k) in p.ops:
                    if e != name:
                        continue
                    for (ch, v) in waits:
                        if hwm.get(ch, -1) < v:
                            eng.wait_ge(sems[ch], v)
                            hwm[ch] = v
                    ins = fn(eng) if fn is not None else None
                    if inc is not None:
                        if ins is None:
                            ins = eng.nop()
                        ins.then_inc(sems[inc], k)
            return go

        block.sync(emit_for("sync"))
        block.scalar(emit_for("scalar"))
        block.tensor(emit_for("tensor"))
        block.vector(emit_for("vector"))
        block.gpsimd(emit_for("gpsimd"))

    es.close()
    return nc



def _hostprep(inputs):
    f32 = np.float32
    x = np.asarray(inputs["L0_attn_hidden_states"], f32)
    dt_w = np.asarray(inputs["L0_attn_dt_w"], f32)
    conv_w = np.asarray(inputs["L0_attn_conv_w"], f32)

    shared = {
        "winT": np.ascontiguousarray(np.asarray(inputs["L0_attn_in_proj_w"], f32).T),
        "wgT": np.ascontiguousarray(np.asarray(inputs["L0_attn_gate_proj_w"], f32).T),
        "woT": np.ascontiguousarray(np.asarray(inputs["L0_attn_out_proj_w"], f32).T),
        "wsgT": np.ascontiguousarray(np.asarray(inputs["L0_ffn_sh_gate"], f32).T),
        "wsuT": np.ascontiguousarray(np.asarray(inputs["L0_ffn_sh_up"], f32).T),
        "wsdT": np.ascontiguousarray(np.asarray(inputs["L0_ffn_sh_down"], f32).T),
        "wrgT": np.ascontiguousarray(np.asarray(inputs["L0_ffn_r_gate"], f32).T),
        "wruT": np.ascontiguousarray(np.asarray(inputs["L0_ffn_r_up"], f32).T),
        "wrdT": np.ascontiguousarray(np.asarray(inputs["L0_ffn_r_down"], f32).T),
        "wrtT": np.ascontiguousarray(np.asarray(inputs["L0_ffn_router_weight"], f32).T),
        "ident": np.eye(128, dtype=f32),
    }
    sdt = np.zeros((NCB, 128, 128), f32)
    dtw_r = dt_w.reshape(NCB, 2, 64)
    for h2 in range(2):
        sdt[:, h2 * 64:(h2 + 1) * 64, h2 * 64:(h2 + 1) * 64] = dtw_r[:, h2][:, :, None]
    shared["sdt"] = np.ascontiguousarray(sdt.transpose(1, 0, 2).reshape(128, NCB * 128))
    z = np.zeros((NCB, 4, 128, 128), f32)
    cw_r = conv_w.reshape(NCB, 128, 4)
    z[:, :, np.arange(128), np.arange(128)] = cw_r.transpose(0, 2, 1)[:, :, :]
    shared["diag"] = np.ascontiguousarray(z.transpose(2, 0, 1, 3).reshape(128, NCB * 4 * 128))

    def chancols(v):
        return np.ascontiguousarray(np.asarray(v, f32).reshape(-1).reshape(NCB, 128).T)

    shared["alog"] = chancols(inputs["L0_attn_A_log"])
    shared["bp"] = chancols(inputs["L0_attn_Bp"])
    shared["cp"] = chancols(inputs["L0_attn_Cp"])
    shared["dp"] = chancols(inputs["L0_attn_Dp"])
    shared["convb"] = chancols(inputs["L0_attn_conv_b"])
    shared["dtbc"] = np.ascontiguousarray(
        np.repeat(np.asarray(inputs["L0_attn_dt_b"], f32), 64).reshape(NCB, 128).T)

    in_maps = []
    for c in range(8):
        b, q = c // 4, c % 4
        xs = np.zeros((D, TC), f32)
        if q == 0:
            xs[:, HALO:] = x[b, 0:T, :].T
        else:
            xs[:, :] = x[b, q * T - HALO:(q + 1) * T, :].T
        cm = np.zeros((128, 8), f32)
        for j in range(8):
            if j // 4 == b and (j % 4) < q:
                cm[:, j] = 1.0
        m = dict(shared)
        m["x"] = np.ascontiguousarray(xs)
        m["cmask"] = cm
        in_maps.append(m)
    return in_maps


DBG_MODE = False


def kernel(**inputs):
    in_maps = _hostprep(inputs)
    nc = build_nc(dbg=DBG_MODE)
    res = run_bass_kernel_spmd(nc, in_maps, core_ids=list(range(8)))
    out = np.empty((B, L, D), np.float32)
    for c in range(8):
        b, q = c // 4, c % 4
        out[b, q * T:(q + 1) * T, :] = res.results[c]["o"].T
    if DBG_MODE:
        return out, res.results
    return out
